# version 08
# baseline (speedup 1.0000x reference)
import numpy as np
import ml_dtypes
from contextlib import ExitStack

import concourse.bass as bass
import concourse.mybir as mybir
from concourse.bass_utils import run_bass_kernel_spmd

F32 = mybir.dt.float32
BF16 = mybir.dt.bfloat16
I32 = mybir.dt.int32
ALU = mybir.AluOpType
ACT = mybir.ActivationFunctionType
AX = mybir.AxisListType

C = 1024
H = 8
NK = C // 128
R = 32
EPS = 1e-5
ROPE_BASE = 10000.0
P = 128


class Tick:
    def __init__(self, stack, nc, name, step=1):
        self.sem = stack.enter_context(nc.semaphore(name))
        self.step = step

    def inc(self, inst):
        return inst.then_inc(self.sem, self.step)

    def wait(self, eng, k):
        if k > 0:
            eng.wait_ge(self.sem, k * self.step)


def build_nc(T=4096, dbg=False, serial=False):
    nt = T // P
    nc = bass.Bass("TRN2", target_bir_lowering=False,
                   detect_race_conditions=False)

    x_d = nc.dram_tensor("L0_attn_x", [T, C], F32, kind="ExternalInput").ap()
    qkw_d = nc.dram_tensor("L0_attn_qk_w", [C, 2 * C], F32, kind="ExternalInput").ap()
    qkb_d = nc.dram_tensor("L0_attn_qk_b", [2 * C], F32, kind="ExternalInput").ap()
    lepw_d = nc.dram_tensor("L0_attn_lepe_w", [3, C, C], F32, kind="ExternalInput").ap()
    lepb_d = nc.dram_tensor("L0_attn_lepe_b", [C], F32, kind="ExternalInput").ap()
    W_d = nc.dram_tensor("L0_ffn_W", [C, C], F32, kind="ExternalInput").ap()
    b_d = nc.dram_tensor("L0_ffn_b", [C], F32, kind="ExternalInput").ap()
    Vr_d = nc.dram_tensor("L0_ffn_V_r", [C, R], F32, kind="ExternalInput").ap()
    Cm_d = nc.dram_tensor("L0_ffn_C", [C, R], F32, kind="ExternalInput").ap()
    out_d = nc.dram_tensor("out", [T, C], F32, kind="ExternalOutput").ap()
    x1_d = nc.dram_tensor("x1buf", [T, C], F32,
                          kind="ExternalOutput" if dbg else "Internal").ap()
    if dbg:
        dkv_d = nc.dram_tensor("dbg_kv", [P, C], F32, kind="ExternalOutput").ap()
        dkm_d = nc.dram_tensor("dbg_kmb", [P, C], F32, kind="ExternalOutput").ap()
        dbt_d = nc.dram_tensor("dbg_btab", [P, C], F32, kind="ExternalOutput").ap()
    atab_d = nc.dram_tensor("atab_d", [nt, C], BF16,
                            kind="ExternalOutput" if dbg else "Internal").ap()
    kmrow_d = nc.dram_tensor("kmrow_d", [1, C], BF16, kind="Internal").ap()

    st = ExitStack()
    with st:
        sb = lambda n, s, d=F32: st.enter_context(nc.sbuf_tensor(n, s, d))
        tick = lambda n, step=1: Tick(st, nc, n, step)

        Wqk_q = sb("Wqk_q", [P, NK, C], BF16)
        Wlep = sb("Wlep", [P, 3, NK, C], BF16)
        Wc = sb("Wc", [P, NK, C], BF16)
        kv_sb = sb("kv_sb", [P, C], BF16)
        kmb = sb("kmb", [P, C], BF16)
        Btab = sb("Btab", [P, C], BF16)
        bc_row = sb("bc_row", [1, C], BF16)
        qkbq_row = sb("qkbq_row", [1, C], BF16)
        lepb_row = sb("lepb_row", [1, C], BF16)
        ones1 = sb("ones1", [1, P], BF16)
        ones_col = sb("ones_col", [P, 1], BF16)
        ident = sb("ident", [P, P], BF16)
        c_eps = sb("c_eps", [P, 1], F32)
        c_inv64 = sb("c_inv64", [P, 1], F32)
        c_npi = sb("c_npi", [P, 1], F32)
        c_2pi = sb("c_2pi", [P, 1], F32)
        c_s1 = sb("c_s1", [P, 1], F32)

        t_ld = tick("t_ld", 16)
        t_cw = tick("t_cw"); t_cq = tick("t_cq"); t_cl = tick("t_cl")
        t_misc = tick("t_misc"); t_tp = tick("t_tp"); t_tv = tick("t_tv")
        t_tab = tick("t_tab"); t_ab = tick("t_ab", 16)
        t_gld = tick("t_gld", 16)
        t_ldx = tick("t_ldx", 16)
        t_sq0 = tick("t_sq0"); t_st0 = tick("t_st0"); t_sqrt0 = tick("t_sqrt0")
        t_is0 = tick("t_is0"); t_xn0 = tick("t_xn0")
        t_T0 = tick("t_T0"); t_eT0 = tick("t_eT0")
        t_ffn = tick("t_ffn"); t_x1 = tick("t_x1"); t_stx1 = tick("t_stx1", 16)
        t_sq1 = tick("t_sq1"); t_st1 = tick("t_st1"); t_sqrt1 = tick("t_sqrt1")
        t_is1 = tick("t_is1"); t_xn1 = tick("t_xn1")
        t_T1 = tick("t_T1"); t_eT1 = tick("t_eT1")
        t_qkk = tick("t_qkk"); t_emin = tick("t_emin"); t_exp = tick("t_exp")
        t_estt = tick("t_estt"); t_csbc = tick("t_csbc", 16); t_cs6 = tick("t_cs6")
        t_rope = tick("t_rope"); t_kv = tick("t_kv"); t_ks = tick("t_ks")
        t_aend = tick("t_aend"); t_kmbd = tick("t_kmbd", 16)
        u_ldx = tick("u_ldx", 16)
        u_sq1 = tick("u_sq1"); u_st1 = tick("u_st1"); u_sqrt1 = tick("u_sqrt1")
        u_is1 = tick("u_is1"); u_xn1 = tick("u_xn1")
        u_T1 = tick("u_T1"); u_eT1 = tick("u_eT1")
        u_qkq = tick("u_qkq"); u_emin = tick("u_emin"); u_exp = tick("u_exp")
        u_estt = tick("u_estt"); u_csbc = tick("u_csbc", 16); u_cs6 = tick("u_cs6")
        u_rope = tick("u_rope"); u_Tq = tick("u_Tq"); u_eTq = tick("u_eTq")
        u_dots = tick("u_dots"); u_attn = tick("u_attn"); u_lepe = tick("u_lepe")
        u_epi1 = tick("u_epi1"); u_epi2 = tick("u_epi2")
        u_sq2 = tick("u_sq2"); u_st2 = tick("u_st2"); u_sqrt2 = tick("u_sqrt2")
        u_is2 = tick("u_is2"); u_xn2 = tick("u_xn2")
        u_T2 = tick("u_T2"); u_eT2 = tick("u_eT2")
        u_ffn2 = tick("u_ffn2"); u_x3 = tick("u_x3"); u_out = tick("u_out", 16)
        u_pad = tick("u_pad")
        t_kev = tick("t_kev"); u_qev = tick("u_qev")
        vself = tick("vself")
        _vc = [0]

        def vg(eng, inst):
            _vc[0] += 1
            inst.then_inc(vself.sem, 1)
            eng.wait_ge(vself.sem, _vc[0])
            return inst

        with ExitStack() as ast:
            asb = lambda n, s, d=F32: ast.enter_context(nc.sbuf_tensor(n, s, d))
            aps = lambda n, s, d=F32: ast.enter_context(nc.psum_tensor(n, s, d))

            kmrow_sb = asb("kmrow_sb", [H, P], BF16)
            ksf = asb("ksf", [P, H], F32)
            ksbf = asb("ksbf", [P, H], BF16)
            Wqk_k = asb("Wqk_k", [P, NK, C], BF16)
            qkbk_row = asb("qkbk_row", [1, C], BF16)
            xa = asb("xa", [P, 4, C], F32)
            xn0b = asb("xn0b", [P, 2, C], BF16)
            xn0T = asb("xn0T", [P, 2, C], BF16)
            xn1b = asb("xn1b", [P, 4, C], BF16)
            xn1T = asb("xn1T", [P, 2, C], BF16)
            mnb = asb("mnb", [P, C], BF16)
            eb = asb("eb", [P, C], BF16)
            kpb = asb("kpb", [P, 2, C], BF16)
            krb = asb("krb", [P, 2, C], BF16)
            cab = asb("cab", [P, 2, C], BF16)
            cst = asb("cst", [P, 2, C], BF16)
            vtmp = asb("vtmp", [P, C], BF16)
            sqg = asb("sqg", [P, C], BF16)
            sm = asb("sm", [P, 8], F32)
            sn = asb("sn", [P, 8], F32)
            sm0 = asb("sm0", [P, 8], F32)
            sn0 = asb("sn0", [P, 8], F32)

            p_y = aps("p_y", [P, C], F32)
            p_k = aps("p_k", [P, C], F32)
            p_kv = aps("p_kv", [P, C], F32)
            p_ks = aps("p_ks", [P, H], F32)
            p_t = aps("p_t", [P, C], BF16)

            sst = ExitStack()
            ssb = lambda n, s, d=F32: sst.enter_context(nc.sbuf_tensor(n, s, d))
            stg = ssb("stg", [P, 2, C], F32)
            wtmp = ssb("wtmp", [P, C], BF16)
            VrS = ssb("VrS", [P, NK * R], BF16)
            CS = ssb("CS", [P, NK * R], BF16)
            VrT = ssb("VrT", [R, C], BF16)
            CT = ssb("CT", [R, C], BF16)
            idn = ssb("idn", [P, P], I32)
            tabf = ssb("tabf", [P, C], F32)
            tabi = ssb("tabi", [P, 512], I32)
            tabs2 = ssb("tabs2", [P, 512], F32)
            tabk = ssb("tabk", [P, C], BF16)
            atf = ssb("atf", [nt, C], BF16)
            throw = ssb("throw", [1, C], F32)
            posr = ssb("posr", [1, P], F32)
            tbr = ssb("tbr", [1, nt], F32)
            rows4 = ssb("rows4", [1, 2 * C], BF16)
            c_half = ssb("c_half", [P, 1], F32)

            with nc.Block() as blk:
                @blk.gpsimd
                def _(eng):
                    eng.iota(idn[:, :], [[1, P]], channel_multiplier=-1,
                             allow_small_or_imprecise_dtypes=True)
                    eng.iota(throw[:, 0:512], [[1, 512]], channel_multiplier=0,
                             allow_small_or_imprecise_dtypes=True)
                    eng.iota(posr[:, :], [[1, P]], channel_multiplier=0,
                             allow_small_or_imprecise_dtypes=True)
                    eng.iota(tbr[:, :], [[P, nt]], channel_multiplier=0,
                             allow_small_or_imprecise_dtypes=True)
                    eng.memset(ones1[:, :], 1.0)
                    eng.memset(ones_col[:, :], 1.0)
                    eng.memset(c_eps[:, :], EPS)
                    eng.memset(c_inv64[:, :], 1.0 / float(np.sqrt(T)))
                    eng.memset(c_npi[:, :], -float(np.pi))
                    eng.memset(c_2pi[:, :], 2.0 * float(np.pi))
                    eng.memset(c_s1[:, :], -float(np.log(ROPE_BASE)) / 512.0)
                    eng.memset(c_half[:, :], 0.5)
                    eng.memset(throw[:, 512:C], 0.0)
            nc.all_engine_barrier()

            with nc.Block() as blk:
                @blk.sync
                def _(eng):
                    eng.dma_start(stg[:, 0, 0:NK * R],
                                  bass.AP(Vr_d.tensor, 0,
                                          [[R, P], [P * R, NK], [1, R]])
                                  ).then_inc(t_ld.sem, 16)
                    eng.dma_start(stg[:, 1, 0:NK * R],
                                  bass.AP(Cm_d.tensor, 0,
                                          [[R, P], [P * R, NK], [1, R]])
                                  ).then_inc(t_ld.sem, 16)
                    for k in range(NK):
                        if k < 2:
                            t_misc.wait(eng, 2)
                        else:
                            t_cw.wait(eng, k - 1)
                        eng.dma_start(stg[:, k % 2, :],
                                      W_d[k * P:(k + 1) * P, :]
                                      ).then_inc(t_ld.sem, 16)
                    for h2 in range(2):
                        for k in range(NK):
                            i = h2 * NK + k
                            if i < 2:
                                t_cw.wait(eng, NK)
                            else:
                                t_cq.wait(eng, i - 1)
                            eng.dma_start(stg[:, i % 2, :],
                                          qkw_d[k * P:(k + 1) * P,
                                                h2 * C:(h2 + 1) * C]
                                          ).then_inc(t_ld.sem, 16)
                    for i in range(3 * NK):
                        tap, k = divmod(i, NK)
                        if i < 2:
                            t_cq.wait(eng, 2 * NK)
                        else:
                            t_cl.wait(eng, i - 1)
                        eng.dma_start(stg[:, i % 2, :],
                                      lepw_d[tap, k * P:(k + 1) * P, :]
                                      ).then_inc(t_ld.sem, 16)

                @blk.vector
                def _(eng):
                    t_ld.wait(eng, 2)
                    t_gld.wait(eng, 2)
                    eng.tensor_scalar_mul(VrS[:, :], stg[:, 0, 0:NK * R], 0.5)
                    eng.tensor_copy(CS[:, :], stg[:, 1, 0:NK * R])
                    eng.tensor_scalar_mul(bc_row[:, :], rows4[0:1, 0:C], 0.5)
                    eng.tensor_copy(lepb_row[:, :], rows4[0:1, C:2 * C])
                    vg(eng, eng.tensor_single_scalar(ident[:, :], idn[:, :],
                                                     0.0, ALU.is_equal))
                    t_misc.inc(eng.tensor_copy(tabk[:, 0:P], ident[:, :]))
                    t_gld.wait(eng, 3)
                    eng.tensor_copy(
                        qkbq_row[:, :],
                        rows4[0:1, 0:C].rearrange("p (n two) -> p n two",
                                                  two=2).transpose([0, 2, 1]))
                    t_misc.inc(eng.tensor_copy(
                        qkbk_row[:, :],
                        rows4[0:1, C:2 * C].rearrange("p (n two) -> p n two",
                                                      two=2).transpose([0, 2, 1])))

                @blk.gpsimd
                def _(eng):
                    eng.dma_start(rows4[0:1, 0:C], b_d[None, :]).then_inc(t_gld.sem, 16)
                    eng.dma_start(rows4[0:1, C:2 * C],
                                  lepb_d[None, :]).then_inc(t_gld.sem, 16)
                    t_misc.wait(eng, 1)
                    eng.dma_start(rows4[0:1, 0:2 * C],
                                  qkb_d[None, :]).then_inc(t_gld.sem, 16)
                    for h2 in range(2):
                        dst = Wqk_q if h2 == 0 else Wqk_k
                        for k in range(NK):
                            i = h2 * NK + k
                            t_ld.wait(eng, NK + 2 + i + 1)
                            t_cq.inc(eng.tensor_copy(
                                dst[:, k, :],
                                stg[:, i % 2, :].rearrange(
                                    "p (n two) -> p n two", two=2
                                    ).transpose([0, 2, 1])))

                @blk.tensor
                def _(eng):
                    t_misc.wait(eng, 1)
                    for a in range(NK):
                        inst = eng.transpose(p_t[0:R, a * P:(a + 1) * P],
                                             VrS[:, a * R:(a + 1) * R],
                                             ident[:, :])
                    t_tp.inc(inst)

                @blk.vector
                def _(eng):
                    t_tp.wait(eng, 1)
                    t_tv.inc(eng.tensor_copy(VrT[:, :], p_t[0:R, :]))

                @blk.tensor
                def _(eng):
                    t_tv.wait(eng, 1)
                    for a in range(NK):
                        inst = eng.transpose(p_t[0:R, a * P:(a + 1) * P],
                                             CS[:, a * R:(a + 1) * R],
                                             ident[:, :])
                    t_tp.inc(inst)

                @blk.vector
                def _(eng):
                    t_tp.wait(eng, 2)
                    t_tv.inc(eng.tensor_copy(CT[:, :], p_t[0:R, :]))

                for r2 in range(NK):
                    @blk.vector
                    def _(eng, r2=r2):
                        t_ld.wait(eng, 2 + r2 + 1)
                        if r2 > 0:
                            t_tp.wait(eng, 2 + r2)
                        t_cw.inc(eng.tensor_copy(wtmp[:, :], stg[:, r2 % 2, :]))

                    @blk.tensor
                    def _(eng, r2=r2):
                        t_cw.wait(eng, r2 + 1)
                        if r2 == 0:
                            t_tv.wait(eng, 2)
                        else:
                            t_tv.wait(eng, 2 + r2)
                        for cch in range(NK):
                            inst = eng.transpose(p_t[:, cch * P:(cch + 1) * P],
                                                 wtmp[:, cch * P:(cch + 1) * P],
                                                 ident[:, :])
                        t_tp.inc(inst)

                    @blk.scalar
                    def _(eng, r2=r2):
                        t_tp.wait(eng, 3 + r2)
                        t_tv.inc(eng.activation(
                            Wc[:, :, r2 * P:(r2 + 1) * P],
                            p_t[:, :].rearrange("p (k f) -> p k f", k=NK),
                            ACT.Copy, scale=c_half[:, :]))

                if True:
                    p_w = p_y
                    for cch in range(NK):
                        @blk.tensor
                        def _(eng, cch=cch):
                            t_tv.wait(eng, 2 + NK + cch)
                            eng.matmul(p_w[:, 0:512],
                                       VrT[:, cch * P:(cch + 1) * P],
                                       CT[:, 0:512], start=True, stop=True)
                            t_tp.inc(eng.matmul(p_w[:, 512:C],
                                                VrT[:, cch * P:(cch + 1) * P],
                                                CT[:, 512:C],
                                                start=True, stop=True))

                        @blk.vector
                        def _(eng, cch=cch):
                            t_tp.wait(eng, 3 + NK + cch)
                            t_tv.inc(eng.tensor_tensor(Wc[:, cch, :],
                                                       Wc[:, cch, :],
                                                       p_w[:, :], ALU.add))

                    @blk.vector
                    def _(eng):
                        for i in range(3 * NK):
                            tap, k = divmod(i, NK)
                            t_ld.wait(eng, 2 * NK + 2 + NK + i + 1)
                            t_cl.inc(eng.tensor_copy(Wlep[:, tap, k, :],
                                                     stg[:, i % 2, :]))

                    @blk.scalar
                    def _(eng):
                        t_tab.inc(eng.activation(throw[0:1, 0:512],
                                                 throw[0:1, 0:512], ACT.Exp,
                                                 scale=c_s1[0:1, :]))

                    @blk.vector
                    def _(eng):
                        t_tab.wait(eng, 1)
                        t_tab.inc(eng.tensor_scalar_mul(
                            throw[0:1, 0:512], throw[0:1, 0:512],
                            1.0 / (2.0 * float(np.pi))))

                    @blk.tensor
                    def _(eng):
                        t_tab.wait(eng, 2)
                        t_tv.wait(eng, 2 + 2 * NK)
                        eng.matmul(p_w[0:nt, 0:512], tbr[:, :],
                                   throw[0:1, 0:512], start=True, stop=True)
                        t_tab.inc(eng.matmul(p_w[0:nt, 512:C], tbr[:, :],
                                             throw[0:1, 0:512],
                                             start=True, stop=True))

                    def rangered(eng, pang, rows):
                        vg(eng, eng.tensor_scalar_add(tabf[0:rows, 0:512],
                                                      pang[0:rows, 0:512], 0.25))
                        vg(eng, eng.tensor_scalar_add(tabf[0:rows, 512:C],
                                                      pang[0:rows, 512:C], 0.0))
                        for hh in range(2):
                            slh = slice(512 * hh, 512 * (hh + 1))
                            vg(eng, eng.tensor_copy(tabi[0:rows, :],
                                                    tabf[0:rows, slh]))
                            vg(eng, eng.tensor_copy(tabs2[0:rows, :],
                                                    tabi[0:rows, :]))
                            vg(eng, eng.tensor_sub(tabf[0:rows, slh],
                                                   tabf[0:rows, slh],
                                                   tabs2[0:rows, :]))
                        vg(eng, eng.tensor_single_scalar(tabs2[0:rows, :],
                                                         tabf[0:rows, 0:512],
                                                         0.5, ALU.is_ge))
                        vg(eng, eng.tensor_sub(tabf[0:rows, 0:512],
                                               tabf[0:rows, 0:512],
                                               tabs2[0:rows, :]))
                        vg(eng, eng.tensor_single_scalar(tabs2[0:rows, :],
                                                         tabf[0:rows, 512:C],
                                                         0.5, ALU.is_ge))
                        vg(eng, eng.tensor_sub(tabf[0:rows, 512:C],
                                               tabf[0:rows, 512:C],
                                               tabs2[0:rows, :]))
                        vg(eng, eng.tensor_single_scalar(tabf[0:rows, :],
                                                         tabf[0:rows, :],
                                                         -0.4999995, ALU.max))
                        vg(eng, eng.tensor_single_scalar(tabf[0:rows, :],
                                                         tabf[0:rows, :],
                                                         0.4999995, ALU.min))

                    @blk.vector
                    def _(eng):
                        t_tab.wait(eng, 3)
                        rangered(eng, p_w, nt)
                        t_tab.inc(eng.engine_nop())

                    @blk.scalar
                    def _(eng):
                        t_tab.wait(eng, 4)
                        t_tab.inc(eng.activation(atf[:, :], tabf[0:nt, :],
                                                 ACT.Sin,
                                                 scale=c_2pi[0:nt, :]))

                    @blk.sync
                    def _(eng):
                        t_tab.wait(eng, 5)
                        eng.dma_start(atab_d, atf[:, :]).then_inc(t_ab.sem, 16)

                    @blk.tensor
                    def _(eng):
                        t_tab.wait(eng, 5)
                        eng.matmul(p_w[:, 0:512], posr[:, :], throw[0:1, 0:512],
                                   start=True, stop=True)
                        t_tab.inc(eng.matmul(p_w[:, 512:C], posr[:, :],
                                             throw[0:1, 0:512],
                                             start=True, stop=True))

                    @blk.vector
                    def _(eng):
                        t_tab.wait(eng, 6)
                        rangered(eng, p_w, P)
                        t_tab.inc(eng.engine_nop())

                    @blk.scalar
                    def _(eng):
                        t_tab.wait(eng, 7)
                        t_tab.inc(eng.activation(Btab[:, :], tabf[:, :],
                                                 ACT.Sin,
                                                 scale=c_2pi[:, :]))

                @blk.vector
                def _(eng):
                    eng.memset(p_kv[:, :], 0.0)
                    eng.memset(p_ks[:, :], 0.0)

            sst.close()
            nc.all_engine_barrier()

            def A_iter(blk, i):
                jl, j2, j1, j = i + 3, i + 2, i + 1, i
                j3 = i + 3
                jm, jm2, jm3 = i - 1, i - 2, i - 3

                @blk.sync
                def _(eng):
                    if 0 <= jl < nt:
                        if jl >= 4:
                            t_stx1.wait(eng, jl - 3)
                            t_xn1.wait(eng, jl - 3)
                        t_ldx.inc(eng.dma_start(xa[:, jl % 4, :],
                                                x_d[jl * P:(jl + 1) * P, :]))
                    if 0 <= j < nt:
                        t_ab.wait(eng, 1)
                        if j >= 2:
                            t_cs6.wait(eng, j - 1)
                        t_csbc.inc(eng.dma_start(
                            cab[:, j % 2, :],
                            bass.AP(atab_d.tensor, j * C, [[0, P], [1, C]])))
                    if 0 <= j < nt:
                        t_x1.wait(eng, j + 1)
                        t_stx1.inc(eng.dma_start(x1_d[j * P:(j + 1) * P, :],
                                                 xa[:, j % 4, :]))

                @blk.vector
                def _(eng):
                    if 0 <= jm2 < nt:
                        t_kev.wait(eng, jm2 + 1)
                        t_exp.wait(eng, jm2)
                        t_emin.inc(eng.tensor_single_scalar(
                            mnb[:, :], kpb[:, jm2 % 2, :], 0.0, ALU.min))
                    if 0 <= j < nt:
                        t_ffn.wait(eng, j + 1)
                        t_xn0.wait(eng, j + 1)
                        t_x1.inc(eng.tensor_tensor(xa[:, j % 4, :],
                                                   xa[:, j % 4, :], p_y[:, :],
                                                   ALU.add))
                    if 0 <= j < nt:
                        q = j % 2
                        t_xn1.wait(eng, j - 1)
                        vg(eng, eng.tensor_reduce(sm[:, q:q + 1],
                                                  xa[:, j % 4, :],
                                                  AX.X, ALU.add))
                        vg(eng, eng.tensor_scalar_mul(sm[:, 2 + q:3 + q],
                                                      sm[:, q:q + 1], 1.0 / C))
                        t_sq1.wait(eng, j + 1)
                        vg(eng, eng.tensor_mul(sm[:, 4 + q:5 + q],
                                               sm[:, 2 + q:3 + q],
                                               sm[:, 2 + q:3 + q]))
                        t_st1.inc(eng.scalar_tensor_tensor(
                            sm[:, 4 + q:5 + q], sn[:, q:q + 1], 1.0 / C,
                            sm[:, 4 + q:5 + q], ALU.mult, ALU.subtract))
                    if 0 <= j < nt:
                        q = j % 2
                        t_sqrt1.wait(eng, j + 1)
                        vg(eng, eng.reciprocal(sm[:, 6 + q:7 + q],
                                               sn[:, 6 + q:7 + q]))
                        t_is1.inc(eng.tensor_scalar(
                            sn[:, 2 + q:3 + q], sm[:, 2 + q:3 + q],
                            sm[:, 6 + q:7 + q], -1.0, ALU.mult, ALU.mult))
                    if 0 <= jm2 < nt:
                        t_exp.wait(eng, jm2 + 1)
                        t_estt.inc(eng.scalar_tensor_tensor(
                            kpb[:, jm2 % 2, :], kpb[:, jm2 % 2, :], 0.0,
                            eb[:, :], ALU.max, ALU.add))
                    if 0 <= jm < nt:
                        t_csbc.wait(eng, jm + 1)
                        ca = cab[:, jm % 2, :]
                        cd = cst[:, jm % 2, :]
                        eng.tensor_mul(vtmp[:, 0:512], ca[:, 0:512], Btab[:, 0:512])
                        eng.tensor_mul(vtmp[:, 512:C], ca[:, 512:C], Btab[:, 512:C])
                        eng.tensor_sub(cd[:, 0:512], vtmp[:, 0:512],
                                       vtmp[:, 512:C])
                        eng.tensor_mul(vtmp[:, 0:512], ca[:, 512:C], Btab[:, 0:512])
                        eng.tensor_mul(vtmp[:, 512:C], ca[:, 0:512], Btab[:, 512:C])
                        t_cs6.inc(eng.tensor_add(cd[:, 512:C], vtmp[:, 0:512],
                                                 vtmp[:, 512:C]))
                    if 0 <= jm2 < nt:
                        t_kv.wait(eng, jm2 - 1)
                        cs = cst[:, jm2 % 2, :]
                        kp = kpb[:, jm2 % 2, :]
                        eng.tensor_mul(vtmp[:, 0:512], cs[:, 0:512], kp[:, 0:512])
                        eng.tensor_mul(vtmp[:, 512:C], cs[:, 512:C], kp[:, 512:C])
                        eng.tensor_sub(krb[:, jm2 % 2, 0:512], vtmp[:, 0:512],
                                       vtmp[:, 512:C])
                        eng.tensor_mul(vtmp[:, 0:512], cs[:, 0:512], kp[:, 512:C])
                        eng.tensor_mul(vtmp[:, 512:C], cs[:, 512:C], kp[:, 0:512])
                        t_rope.inc(eng.tensor_add(krb[:, jm2 % 2, 512:C],
                                                  vtmp[:, 0:512],
                                                  vtmp[:, 512:C]))
                    if 0 <= j3 < nt:
                        q = j3 % 2
                        t_ldx.wait(eng, j3 + 1)
                        t_xn0.wait(eng, j3 - 1)
                        vg(eng, eng.tensor_reduce(sm0[:, q:q + 1],
                                                  xa[:, j3 % 4, :],
                                                  AX.X, ALU.add))
                        vg(eng, eng.tensor_scalar_mul(sm0[:, 2 + q:3 + q],
                                                      sm0[:, q:q + 1], 1.0 / C))
                        t_sq0.wait(eng, j3 + 1)
                        vg(eng, eng.tensor_mul(sm0[:, 4 + q:5 + q],
                                               sm0[:, 2 + q:3 + q],
                                               sm0[:, 2 + q:3 + q]))
                        t_st0.inc(eng.scalar_tensor_tensor(
                            sm0[:, 4 + q:5 + q], sn0[:, q:q + 1], 1.0 / C,
                            sm0[:, 4 + q:5 + q], ALU.mult, ALU.subtract))
                    if 0 <= j3 < nt:
                        q = j3 % 2
                        t_sqrt0.wait(eng, j3 + 1)
                        vg(eng, eng.reciprocal(sm0[:, 6 + q:7 + q],
                                               sn0[:, 6 + q:7 + q]))
                        t_is0.inc(eng.tensor_scalar(
                            sn0[:, 2 + q:3 + q], sm0[:, 2 + q:3 + q],
                            sm0[:, 6 + q:7 + q], -1.0, ALU.mult, ALU.mult))

                @blk.scalar
                def _(eng):
                    if 0 <= j1 < nt:
                        t_T0.wait(eng, j1 + 1)
                        t_ffn.wait(eng, j1 - 1)
                        t_eT0.inc(eng.activation(xn0T[:, j1 % 2, :], p_t[:, :],
                                                 ACT.Copy))
                    if 0 <= jm < nt:
                        t_T1.wait(eng, jm + 1)
                        t_qkk.wait(eng, jm - 1)
                        t_eT1.inc(eng.activation(xn1T[:, jm % 2, :], p_t[:, :],
                                                 ACT.Copy))
                    if 0 <= jm < nt:
                        t_qkk.wait(eng, jm + 1)
                        t_estt.wait(eng, jm - 1)
                        t_ks.wait(eng, jm - 1)
                        t_kev.inc(eng.activation(kpb[:, jm % 2, :], p_k[:, :],
                                                 ACT.Copy))
                    if 0 <= jm2 < nt:
                        t_emin.wait(eng, jm2 + 1)
                        t_estt.wait(eng, jm2)
                        t_exp.inc(eng.activation(eb[:, :], mnb[:, :], ACT.Exp))
                    if 0 <= j < nt:
                        q = j % 2
                        t_x1.wait(eng, j + 1)
                        t_st1.wait(eng, j - 1)
                        t_sq1.inc(eng.activation(sqg[:, :], xa[:, j % 4, :],
                                                 ACT.Square,
                                                 accum_out=sn[:, q:q + 1]))
                    if 0 <= j3 < nt:
                        q = j3 % 2
                        t_ldx.wait(eng, j3 + 1)
                        t_st0.wait(eng, j3 - 1)
                        t_sq0.inc(eng.activation(sqg[:, :], xa[:, j3 % 4, :],
                                                 ACT.Square,
                                                 accum_out=sn0[:, q:q + 1]))
                    if 0 <= j < nt:
                        q = j % 2
                        t_st1.wait(eng, j + 1)
                        t_is1.wait(eng, j - 1)
                        t_sqrt1.inc(eng.activation(sn[:, 6 + q:7 + q],
                                                   sm[:, 4 + q:5 + q],
                                                   ACT.Sqrt, bias=c_eps[:, :]))
                    if 0 <= j < nt:
                        q = j % 2
                        t_is1.wait(eng, j + 1)
                        t_kv.wait(eng, j - 3)
                        t_xn1.inc(eng.activation(xn1b[:, j % 4, :],
                                                 xa[:, j % 4, :],
                                                 ACT.Identity,
                                                 bias=sn[:, 2 + q:3 + q],
                                                 scale=sm[:, 6 + q:7 + q]))
                    if 0 <= j3 < nt:
                        q = j3 % 2
                        t_st0.wait(eng, j3 + 1)
                        t_is0.wait(eng, j3 - 1)
                        t_sqrt0.inc(eng.activation(sn0[:, 6 + q:7 + q],
                                                   sm0[:, 4 + q:5 + q],
                                                   ACT.Sqrt, bias=c_eps[:, :]))
                    if 0 <= j3 < nt:
                        q = j3 % 2
                        t_is0.wait(eng, j3 + 1)
                        t_T0.wait(eng, j3 - 1)
                        t_xn0.inc(eng.activation(xn0b[:, q, :], xa[:, j3 % 4, :],
                                                 ACT.Identity,
                                                 bias=sn0[:, 2 + q:3 + q],
                                                 scale=sm0[:, 6 + q:7 + q]))

                @blk.tensor
                def _(eng):
                    if 0 <= j < nt:
                        t_eT0.wait(eng, j + 1)
                        t_x1.wait(eng, j)
                        for h2 in range(2):
                            sl = slice(512 * h2, 512 * (h2 + 1))
                            for k in range(NK):
                                eng.matmul(p_y[:, sl],
                                           xn0T[:, j % 2, k * P:(k + 1) * P],
                                           Wc[:, k, sl],
                                           start=(k == 0), stop=False)
                            inst = eng.matmul(p_y[:, sl], ones1[:, :],
                                              bc_row[:, sl],
                                              start=False, stop=True)
                        t_ffn.inc(inst)
                    if 0 <= j1 < nt:
                        t_xn0.wait(eng, j1 + 1)
                        t_eT1.wait(eng, j1 - 2)
                        for k in range(NK):
                            inst = eng.transpose(p_t[:, k * P:(k + 1) * P],
                                                 xn0b[:, j1 % 2,
                                                      k * P:(k + 1) * P],
                                                 ident[:, :])
                        t_T0.inc(inst)
                    if 0 <= jm3 < nt:
                        t_rope.wait(eng, jm3 + 1)
                        t_xn1.wait(eng, jm3 + 1)
                        for h in range(H):
                            inst = eng.matmul(p_kv[:, h * P:(h + 1) * P],
                                              krb[:, jm3 % 2, h * P:(h + 1) * P],
                                              xn1b[:, jm3 % 4, h * P:(h + 1) * P],
                                              start=False, stop=True,
                                              skip_group_check=True)
                        t_kv.inc(inst)
                    if 0 <= jm3 < nt:
                        t_estt.wait(eng, jm3 + 1)
                        for k in range(NK):
                            inst = eng.matmul(p_ks[:, k:k + 1],
                                              kpb[:, jm3 % 2, k * P:(k + 1) * P],
                                              ones_col[:, :],
                                              start=False, stop=True,
                                              skip_group_check=True)
                        t_ks.inc(inst)
                    if 0 <= jm < nt:
                        t_xn1.wait(eng, jm + 1)
                        t_eT0.wait(eng, min(jm + 3, nt))
                        for k in range(NK):
                            inst = eng.transpose(p_t[:, k * P:(k + 1) * P],
                                                 xn1b[:, jm % 4,
                                                      k * P:(k + 1) * P],
                                                 ident[:, :])
                        t_T1.inc(inst)
                    if 0 <= jm < nt:
                        t_eT1.wait(eng, jm + 1)
                        t_kev.wait(eng, jm)
                        for h2 in range(2):
                            sl = slice(512 * h2, 512 * (h2 + 1))
                            for k in range(NK):
                                eng.matmul(p_k[:, sl],
                                           xn1T[:, jm % 2, k * P:(k + 1) * P],
                                           Wqk_k[:, k, sl],
                                           start=(k == 0), stop=False)
                            inst = eng.matmul(p_k[:, sl], ones1[:, :],
                                              qkbk_row[:, sl],
                                              start=False, stop=True)
                        t_qkk.inc(inst)

            if serial:
                for it in range(-3, nt + 3):
                    with nc.Block() as blk:
                        A_iter(blk, it)
                    nc.all_engine_barrier()
                blk = nc.Block().__enter__()
            else:
                blk = nc.Block().__enter__()
                for it in range(-3, nt + 3):
                    A_iter(blk, it)
            if True:

                @blk.scalar
                def _(eng):
                    t_kv.wait(eng, nt)
                    eng.activation(kv_sb[:, 0:512], p_kv[:, 0:512], ACT.Copy,
                                   scale=c_inv64[:, :])
                    t_aend.inc(eng.activation(kv_sb[:, 512:C], p_kv[:, 512:C],
                                              ACT.Copy, scale=c_inv64[:, :]))

                @blk.vector
                def _(eng):
                    t_ks.wait(eng, nt)
                    vg(eng, eng.tensor_scalar_mul(ksf[:, :], p_ks[:, :], 1.0 / T))
                    t_aend.inc(eng.tensor_copy(ksbf[:, :], ksf[:, :]))

                @blk.tensor
                def _(eng):
                    t_aend.wait(eng, 2)
                    t_eT1.wait(eng, nt)
                    t_aend.inc(eng.transpose(p_t[0:H, 0:P], ksbf[:, :],
                                             ident[:, :]))

                @blk.scalar
                def _(eng):
                    t_aend.wait(eng, 3)
                    t_aend.inc(eng.activation(kmrow_sb[:, :], p_t[0:H, 0:P],
                                              ACT.Copy))

                @blk.sync
                def _(eng):
                    t_aend.wait(eng, 4)
                    eng.dma_start(kmrow_d.rearrange("one (p f) -> (one p) f",
                                                     p=H),
                                  kmrow_sb[:, :]).then_inc(t_kmbd.sem, 16)
                    t_kmbd.wait(eng, 1)
                    eng.dma_start(kmb[:, :],
                                  bass.AP(kmrow_d.tensor, 0, [[0, P], [1, C]])
                                  ).then_inc(t_kmbd.sem, 16)
                    t_kmbd.wait(eng, 2)
            blk.__exit__(None, None, None)

            nc.all_engine_barrier()

        with ExitStack() as bst:
            bsb = lambda n, s, d=F32: bst.enter_context(nc.sbuf_tensor(n, s, d))
            bps = lambda n, s, d=F32: bst.enter_context(nc.psum_tensor(n, s, d))
            xb = bsb("xb", [P, 4, C], F32)
            xn1c = bsb("xn1c", [P, 2, C], BF16)
            xnT_all = bsb("xnT_all", [P, NK, nt * P + 2], BF16)
            qpb = bsb("qpb", [P, 2, C], BF16)
            mnb2 = bsb("mnb2", [P, C], BF16)
            eb2 = bsb("eb2", [P, C], BF16)
            cab2 = bsb("cab2", [P, 2, C], BF16)
            cst2 = bsb("cst2", [P, 1, C], BF16)
            vtmp2 = bsb("vtmp2", [P, C], BF16)
            qrb = bsb("qrb", [P, C], BF16)
            qrT = bsb("qrT", [P, 2, C], BF16)
            xn2b = bsb("xn2b", [P, 2, C], BF16)
            sqg2 = bsb("sqg2", [P, C], BF16)
            sm2 = bsb("sm2", [P, 8], F32)
            sn2 = bsb("sn2", [P, 8], F32)
            sm3 = bsb("sm3", [P, 8], F32)
            sn3 = bsb("sn3", [P, 8], F32)
            dsg = bsb("dsg", [P, 16], F32)
            zb = bsb("zb", [P, 2 * H], F32)
            xn2T = bsb("xn2T", [P, C], BF16)

            p_qa = bps("p_qa", [P, C], F32)
            p_lep = bps("p_lep", [P, C], F32)
            p_y2 = bps("p_y2", [P, C], F32)
            p_t2 = bps("p_t2", [P, C], BF16)
            p_tq = bps("p_tq", [P, C], BF16)

            def B_iter(blk, i):
                jp, j, ja, jb, jc = i + 1, i, i - 1, i - 2, i - 3

                @blk.sync
                def _(eng):
                    if 0 <= jc < nt:
                        u_x3.wait(eng, jc + 1)
                        u_out.inc(eng.dma_start(out_d[jc * P:(jc + 1) * P, :],
                                                xb[:, jc % 4, :]))
                    if 0 <= jp < nt:
                        if jp >= 4:
                            u_out.wait(eng, jp - 3)
                        u_ldx.inc(eng.dma_start(xb[:, jp % 4, :],
                                                x1_d[jp * P:(jp + 1) * P, :]))
                    if 0 <= j < nt:
                        if j >= 2:
                            u_cs6.wait(eng, j - 1)
                        u_csbc.inc(eng.dma_start(
                            cab2[:, j % 2, :],
                            bass.AP(atab_d.tensor, j * C, [[0, P], [1, C]])))

                @blk.vector
                def _(eng):
                    if i == -1:
                        eng.memset(xnT_all[:, :, 0:1], 0.0)
                        u_pad.inc(eng.memset(
                            xnT_all[:, :, nt * P + 1:nt * P + 2], 0.0))
                    if 0 <= ja < nt:
                        u_qev.wait(eng, ja + 1)
                        u_exp.wait(eng, ja)
                        u_emin.inc(eng.tensor_single_scalar(
                            mnb2[:, :], qpb[:, ja % 2, :], 0.0, ALU.min))
                        u_exp.wait(eng, ja + 1)
                        u_estt.inc(eng.scalar_tensor_tensor(
                            qpb[:, ja % 2, :], qpb[:, ja % 2, :], 0.0,
                            eb2[:, :], ALU.max, ALU.add))
                    if 0 <= ja < nt:
                        u_csbc.wait(eng, ja + 1)
                        ca = cab2[:, ja % 2, :]
                        cd = cst2[:, 0, :]
                        eng.tensor_mul(vtmp2[:, 0:512], ca[:, 0:512],
                                       Btab[:, 0:512])
                        eng.tensor_mul(vtmp2[:, 512:C], ca[:, 512:C],
                                       Btab[:, 512:C])
                        eng.tensor_sub(cd[:, 0:512], vtmp2[:, 0:512],
                                       vtmp2[:, 512:C])
                        eng.tensor_mul(vtmp2[:, 0:512], ca[:, 512:C],
                                       Btab[:, 0:512])
                        eng.tensor_mul(vtmp2[:, 512:C], ca[:, 0:512],
                                       Btab[:, 512:C])
                        u_cs6.inc(eng.tensor_add(cd[:, 512:C], vtmp2[:, 0:512],
                                                 vtmp2[:, 512:C]))
                    if 0 <= ja < nt:
                        q8 = 8 * (ja % 2)
                        eng.tensor_mul(vtmp2[:, :], qpb[:, ja % 2, :], kmb[:, :])
                        vg(eng, eng.tensor_reduce(dsg[:, :],
                                                  vtmp2[:, :].rearrange(
                                                      "p (s w) -> p s w", s=16),
                                                  AX.X, ALU.add))
                        vg(eng, eng.tensor_add(zb[:, q8:q8 + 8], dsg[:, 0:8],
                                               dsg[:, 8:16]))
                        vg(eng, eng.tensor_scalar_add(zb[:, q8:q8 + 8],
                                                      zb[:, q8:q8 + 8], 1e-6))
                        vg(eng, eng.reciprocal(zb[:, q8:q8 + 8],
                                               zb[:, q8:q8 + 8]))
                    if 0 <= ja < nt:
                        u_Tq.wait(eng, ja)
                        q8 = 8 * (ja % 2)
                        cs = cst2[:, 0, :]
                        kp = qpb[:, ja % 2, :]
                        eng.tensor_mul(vtmp2[:, 0:512], cs[:, 0:512], kp[:, 0:512])
                        eng.tensor_mul(vtmp2[:, 512:C], cs[:, 512:C], kp[:, 512:C])
                        eng.tensor_sub(qrb[:, 0:512], vtmp2[:, 0:512],
                                       vtmp2[:, 512:C])
                        eng.tensor_mul(vtmp2[:, 0:512], cs[:, 0:512], kp[:, 512:C])
                        eng.tensor_mul(vtmp2[:, 512:C], cs[:, 512:C], kp[:, 0:512])
                        vg(eng, eng.tensor_add(qrb[:, 512:C], vtmp2[:, 0:512],
                                               vtmp2[:, 512:C]))
                        for h in range(H):
                            inst = eng.tensor_scalar_mul(
                                qrb[:, h * P:(h + 1) * P],
                                qrb[:, h * P:(h + 1) * P],
                                zb[:, q8 + h:q8 + h + 1])
                        u_rope.inc(inst)
                    if 0 <= jb < nt:
                        u_attn.wait(eng, jb + 1)
                        u_epi1.inc(eng.tensor_tensor(xb[:, jb % 4, :],
                                                     xb[:, jb % 4, :],
                                                     p_lep[:, :], ALU.add))
                    if 0 <= jc < nt:
                        u_ffn2.wait(eng, jc + 1)
                        u_x3.inc(eng.tensor_tensor(xb[:, jc % 4, :],
                                                   xb[:, jc % 4, :],
                                                   p_y2[:, :], ALU.add))
                    if 0 <= jp < nt:
                        q = jp % 2
                        u_ldx.wait(eng, jp + 1)
                        vg(eng, eng.tensor_reduce(sm2[:, q:q + 1],
                                                  xb[:, jp % 4, :],
                                                  AX.X, ALU.add))
                        vg(eng, eng.tensor_scalar_mul(sm2[:, 2 + q:3 + q],
                                                      sm2[:, q:q + 1], 1.0 / C))
                        u_sq1.wait(eng, jp + 1)
                        vg(eng, eng.tensor_mul(sm2[:, 4 + q:5 + q],
                                               sm2[:, 2 + q:3 + q],
                                               sm2[:, 2 + q:3 + q]))
                        u_st1.inc(eng.scalar_tensor_tensor(
                            sm2[:, 4 + q:5 + q], sn2[:, q:q + 1], 1.0 / C,
                            sm2[:, 4 + q:5 + q], ALU.mult, ALU.subtract))
                        u_sqrt1.wait(eng, jp + 1)
                        vg(eng, eng.reciprocal(sm2[:, 6 + q:7 + q],
                                               sn2[:, 6 + q:7 + q]))
                        u_is1.inc(eng.tensor_scalar(
                            sn2[:, 2 + q:3 + q], sm2[:, 2 + q:3 + q],
                            sm2[:, 6 + q:7 + q], -1.0, ALU.mult, ALU.mult))
                    if 0 <= jb < nt:
                        q = jb % 2
                        vg(eng, eng.tensor_reduce(sm3[:, q:q + 1],
                                                  xb[:, jb % 4, :],
                                                  AX.X, ALU.add))
                        vg(eng, eng.tensor_scalar_mul(sm3[:, 2 + q:3 + q],
                                                      sm3[:, q:q + 1], 1.0 / C))
                        u_sq2.wait(eng, jb + 1)
                        vg(eng, eng.tensor_mul(sm3[:, 4 + q:5 + q],
                                               sm3[:, 2 + q:3 + q],
                                               sm3[:, 2 + q:3 + q]))
                        u_st2.inc(eng.scalar_tensor_tensor(
                            sm3[:, 4 + q:5 + q], sn3[:, q:q + 1], 1.0 / C,
                            sm3[:, 4 + q:5 + q], ALU.mult, ALU.subtract))
                        u_sqrt2.wait(eng, jb + 1)
                        vg(eng, eng.reciprocal(sm3[:, 6 + q:7 + q],
                                               sn3[:, 6 + q:7 + q]))
                        u_is2.inc(eng.tensor_scalar(
                            sn3[:, 2 + q:3 + q], sm3[:, 2 + q:3 + q],
                            sm3[:, 6 + q:7 + q], -1.0, ALU.mult, ALU.mult))

                @blk.scalar
                def _(eng):
                    if 0 <= j < nt:
                        u_T1.wait(eng, j + 1)
                        u_eT1.inc(eng.activation(
                            xnT_all[:, :, 1 + j * P:1 + (j + 1) * P],
                            p_t2[:, :].rearrange("p (k f) -> p k f", k=NK),
                            ACT.Copy))
                    if 0 <= ja < nt:
                        u_qkq.wait(eng, ja + 1)
                        u_rope.wait(eng, ja - 1)
                        u_qev.inc(eng.activation(qpb[:, ja % 2, :], p_qa[:, :],
                                                 ACT.Copy))
                    if 0 <= ja < nt:
                        u_emin.wait(eng, ja + 1)
                        u_estt.wait(eng, ja)
                        u_exp.inc(eng.activation(eb2[:, :], mnb2[:, :], ACT.Exp))
                    if 0 <= jc < nt:
                        u_T2.wait(eng, jc + 1)
                        u_ffn2.wait(eng, jc)
                        u_eT2.inc(eng.activation(xn2T[:, :], p_t2[:, :],
                                                 ACT.Copy))
                    if 0 <= jp < nt:
                        q = jp % 2
                        u_ldx.wait(eng, jp + 1)
                        u_st1.wait(eng, jp - 1)
                        u_sq1.inc(eng.activation(sqg2[:, :], xb[:, jp % 4, :],
                                                 ACT.Square,
                                                 accum_out=sn2[:, q:q + 1]))
                        u_st1.wait(eng, jp + 1)
                        u_is1.wait(eng, jp - 1)
                        u_sqrt1.inc(eng.activation(sn2[:, 6 + q:7 + q],
                                                   sm2[:, 4 + q:5 + q], ACT.Sqrt,
                                                   bias=c_eps[:, :]))
                        u_is1.wait(eng, jp + 1)
                        u_T1.wait(eng, jp - 1)
                        u_xn1.inc(eng.activation(xn1c[:, q, :], xb[:, jp % 4, :],
                                                 ACT.Identity,
                                                 bias=sn2[:, 2 + q:3 + q],
                                                 scale=sm2[:, 6 + q:7 + q]))
                    if 0 <= jb < nt:
                        q = jb % 2
                        u_epi1.wait(eng, jb + 1)
                        u_st2.wait(eng, jb - 1)
                        u_sq2.inc(eng.activation(sqg2[:, :], xb[:, jb % 4, :],
                                                 ACT.Square,
                                                 accum_out=sn3[:, q:q + 1]))
                        u_st2.wait(eng, jb + 1)
                        u_is2.wait(eng, jb - 1)
                        u_sqrt2.inc(eng.activation(sn3[:, 6 + q:7 + q],
                                                   sm3[:, 4 + q:5 + q], ACT.Sqrt,
                                                   bias=c_eps[:, :]))
                        u_is2.wait(eng, jb + 1)
                        u_T2.wait(eng, jb)
                        u_xn2.inc(eng.activation(xn2b[:, jb % 2, :],
                                                 xb[:, jb % 4, :],
                                                 ACT.Identity,
                                                 bias=sn3[:, 2 + q:3 + q],
                                                 scale=sm3[:, 6 + q:7 + q]))
                    if 0 <= ja < nt:
                        u_Tq.wait(eng, ja + 1)
                        u_attn.wait(eng, ja - 1)
                        u_eTq.inc(eng.activation(qrT[:, ja % 2, :], p_tq[:, :],
                                                 ACT.Copy))

                @blk.tensor
                def _(eng):
                    if 0 <= j < nt:
                        u_xn1.wait(eng, j + 1)
                        u_eT2.wait(eng, j - 3)
                        for k in range(NK):
                            inst = eng.transpose(p_t2[:, k * P:(k + 1) * P],
                                                 xn1c[:, j % 2,
                                                      k * P:(k + 1) * P],
                                                 ident[:, :])
                        u_T1.inc(inst)
                    if 0 <= jb < nt:
                        u_eT1.wait(eng, min(jb + 2, nt))
                        u_pad.wait(eng, 1)
                        u_epi1.wait(eng, jb)
                        for h2 in range(2):
                            sl = slice(512 * h2, 512 * (h2 + 1))
                            first = True
                            for tap in range(3):
                                for k in range(NK):
                                    eng.matmul(
                                        p_lep[:, sl],
                                        xnT_all[:, k,
                                                jb * P + tap:jb * P + tap + P],
                                        Wlep[:, tap, k, sl],
                                        start=first, stop=False)
                                    first = False
                            inst = eng.matmul(p_lep[:, sl], ones1[:, :],
                                              lepb_row[:, sl],
                                              start=False, stop=True)
                        u_lepe.inc(inst)
                    if 0 <= jc < nt:
                        u_xn2.wait(eng, jc + 1)
                        u_eT1.wait(eng, min(jc + 4, nt))
                        for k in range(NK):
                            inst = eng.transpose(p_t2[:, k * P:(k + 1) * P],
                                                 xn2b[:, jc % 2,
                                                      k * P:(k + 1) * P],
                                                 ident[:, :])
                        u_T2.inc(inst)
                    if 0 <= jb < nt:
                        u_eTq.wait(eng, jb + 1)
                        u_lepe.wait(eng, jb + 1)
                        for h in range(H):
                            inst = eng.matmul(p_lep[:, h * P:(h + 1) * P],
                                              qrT[:, jb % 2, h * P:(h + 1) * P],
                                              kv_sb[:, h * P:(h + 1) * P],
                                              start=False, stop=True,
                                              skip_group_check=True)
                        u_attn.inc(inst)
                    if 0 <= jc < nt:
                        u_eT2.wait(eng, jc + 1)
                        u_x3.wait(eng, jc)
                        for h2 in range(2):
                            sl = slice(512 * h2, 512 * (h2 + 1))
                            for k in range(NK):
                                eng.matmul(p_y2[:, sl],
                                           xn2T[:, k * P:(k + 1) * P],
                                           Wc[:, k, sl],
                                           start=(k == 0), stop=False)
                            inst = eng.matmul(p_y2[:, sl], ones1[:, :],
                                              bc_row[:, sl],
                                              start=False, stop=True)
                        u_ffn2.inc(inst)
                    if 0 <= j < nt:
                        u_eT1.wait(eng, j + 1)
                        u_qev.wait(eng, j)
                        for h2 in range(2):
                            sl = slice(512 * h2, 512 * (h2 + 1))
                            for k in range(NK):
                                eng.matmul(p_qa[:, sl],
                                           xnT_all[:, k,
                                                   1 + j * P:1 + (j + 1) * P],
                                           Wqk_q[:, k, sl],
                                           start=(k == 0), stop=False)
                            inst = eng.matmul(p_qa[:, sl], ones1[:, :],
                                              qkbq_row[:, sl],
                                              start=False, stop=True)
                        u_qkq.inc(inst)
                    if 0 <= ja < nt:
                        u_rope.wait(eng, ja + 1)
                        u_eTq.wait(eng, ja)
                        for k in range(NK):
                            inst = eng.transpose(p_tq[:, k * P:(k + 1) * P],
                                                 qrb[:, k * P:(k + 1) * P],
                                                 ident[:, :])
                        u_Tq.inc(inst)

            if serial:
                for it in range(-1, nt + 3):
                    with nc.Block() as blk:
                        B_iter(blk, it)
                    nc.all_engine_barrier()
                blk = nc.Block().__enter__()
            else:
                blk = nc.Block().__enter__()
                for it in range(-1, nt + 3):
                    B_iter(blk, it)
            if True:
                @blk.sync
                def _(eng):
                    u_out.wait(eng, nt)
            blk.__exit__(None, None, None)

    return nc


_NC_CACHE = {}


def kernel(**inputs):
    T = inputs["L0_attn_x"].shape[1]
    B = inputs["L0_attn_x"].shape[0]
    key = (T,)
    if key not in _NC_CACHE:
        _NC_CACHE[key] = build_nc(T=T)
    nc = _NC_CACHE[key]
    names = ["L0_attn_qk_w", "L0_attn_qk_b", "L0_attn_lepe_w", "L0_attn_lepe_b",
             "L0_ffn_W", "L0_ffn_b", "L0_ffn_V_r", "L0_ffn_C"]
    shared = {n: np.ascontiguousarray(np.asarray(inputs[n], dtype=np.float32))
              for n in names}
    x = np.ascontiguousarray(np.asarray(inputs["L0_attn_x"], dtype=np.float32))
    in_maps = [dict(shared, L0_attn_x=x[i]) for i in range(B)]
    res = run_bass_kernel_spmd(nc, in_maps, core_ids=list(range(B)))
    return np.stack([np.asarray(r["out"], dtype=np.float32)
                     for r in res.results], axis=0)


# version 15
# speedup vs baseline: 1.1904x; 1.1904x over previous
import numpy as np
import ml_dtypes
from contextlib import ExitStack

import concourse.bass as bass
import concourse.mybir as mybir
from concourse.bass_utils import run_bass_kernel_spmd

F32 = mybir.dt.float32
BF16 = mybir.dt.bfloat16
I32 = mybir.dt.int32
ALU = mybir.AluOpType
ACT = mybir.ActivationFunctionType
AX = mybir.AxisListType

C = 1024
H = 8
NK = C // 128
R = 32
EPS = 1e-5
ROPE_BASE = 10000.0
P = 128


class Tick:
    def __init__(self, stack, nc, name, step=1):
        self.sem = stack.enter_context(nc.semaphore(name))
        self.step = step

    def inc(self, inst):
        return inst.then_inc(self.sem, self.step)

    def wait(self, eng, k):
        if k > 0:
            eng.wait_ge(self.sem, k * self.step)


def build_nc(T=4096, dbg=False, serial=False):
    nt = T // P
    nc = bass.Bass("TRN2", target_bir_lowering=False,
                   detect_race_conditions=False)

    x_d = nc.dram_tensor("L0_attn_x", [T, C], F32, kind="ExternalInput").ap()
    qkw_d = nc.dram_tensor("L0_attn_qk_w", [C, 2 * C], F32, kind="ExternalInput").ap()
    qkb_d = nc.dram_tensor("L0_attn_qk_b", [2 * C], F32, kind="ExternalInput").ap()
    lepw_d = nc.dram_tensor("L0_attn_lepe_w", [3, C, C], F32, kind="ExternalInput").ap()
    lepb_d = nc.dram_tensor("L0_attn_lepe_b", [C], F32, kind="ExternalInput").ap()
    W_d = nc.dram_tensor("L0_ffn_W", [C, C], F32, kind="ExternalInput").ap()
    b_d = nc.dram_tensor("L0_ffn_b", [C], F32, kind="ExternalInput").ap()
    Vr_d = nc.dram_tensor("L0_ffn_V_r", [C, R], F32, kind="ExternalInput").ap()
    Cm_d = nc.dram_tensor("L0_ffn_C", [C, R], F32, kind="ExternalInput").ap()
    out_d = nc.dram_tensor("out", [T, C], F32, kind="ExternalOutput").ap()
    x1_d = nc.dram_tensor("x1buf", [T, C], F32,
                          kind="ExternalOutput" if dbg else "Internal").ap()
    if dbg:
        dkv_d = nc.dram_tensor("dbg_kv", [P, C], F32, kind="ExternalOutput").ap()
        dkm_d = nc.dram_tensor("dbg_kmb", [P, C], F32, kind="ExternalOutput").ap()
        dbt_d = nc.dram_tensor("dbg_btab", [P, C], F32, kind="ExternalOutput").ap()
    atab_d = nc.dram_tensor("atab_d", [nt, C], BF16,
                            kind="ExternalOutput" if dbg else "Internal").ap()
    kmrow_d = nc.dram_tensor("kmrow_d", [1, C], BF16, kind="Internal").ap()

    st = ExitStack()
    with st:
        sb = lambda n, s, d=F32: st.enter_context(nc.sbuf_tensor(n, s, d))
        tick = lambda n, step=1: Tick(st, nc, n, step)

        Wqk_q = sb("Wqk_q", [P, NK, C], BF16)
        Wlep = sb("Wlep", [P, 3, NK, C], BF16)
        Wc = sb("Wc", [P, NK, C], BF16)
        kv_sb = sb("kv_sb", [P, C], BF16)
        kmb = sb("kmb", [P, C], BF16)
        Btab = sb("Btab", [P, C], BF16)
        bc_row = sb("bc_row", [1, C], BF16)
        qkbq_row = sb("qkbq_row", [1, C], BF16)
        lepb_row = sb("lepb_row", [1, C], BF16)
        ones1 = sb("ones1", [1, P], BF16)
        ones_col = sb("ones_col", [P, 1], BF16)
        ident = sb("ident", [P, P], BF16)
        c_eps = sb("c_eps", [P, 1], F32)
        c_nhalf = sb("c_nhalf", [P, 1], F32)
        c_inv64 = sb("c_inv64", [P, 1], F32)
        c_npi = sb("c_npi", [P, 1], F32)
        c_2pi = sb("c_2pi", [P, 1], F32)
        c_s1 = sb("c_s1", [P, 1], F32)

        t_ld = tick("t_ld", 16)
        t_cw = tick("t_cw"); t_cq = tick("t_cq")
        t_lq = tick("t_lq", 16); t_wb = tick("t_wb")
        t_misc = tick("t_misc"); t_tp = tick("t_tp"); t_tv = tick("t_tv")
        t_tab = tick("t_tab"); t_ab = tick("t_ab", 16)
        t_gld = tick("t_gld", 16)
        t_ldx = tick("t_ldx", 16)
        t_sq0 = tick("t_sq0"); t_st0 = tick("t_st0"); t_sqrt0 = tick("t_sqrt0")
        t_is0 = tick("t_is0"); t_xn0 = tick("t_xn0")
        t_T0 = tick("t_T0"); t_eT0 = tick("t_eT0")
        t_ffn = tick("t_ffn"); t_x1 = tick("t_x1"); t_stx1 = tick("t_stx1", 16)
        t_sq1 = tick("t_sq1"); t_st1 = tick("t_st1"); t_sqrt1 = tick("t_sqrt1")
        t_is1 = tick("t_is1"); t_xn1 = tick("t_xn1")
        t_T1 = tick("t_T1"); t_eT1 = tick("t_eT1")
        t_qkk = tick("t_qkk"); t_emin = tick("t_emin"); t_exp = tick("t_exp")
        t_estt = tick("t_estt"); t_csbc = tick("t_csbc", 16); t_cs6 = tick("t_cs6")
        t_rope = tick("t_rope"); t_kv = tick("t_kv"); t_ks = tick("t_ks")
        t_aend = tick("t_aend"); t_kmbd = tick("t_kmbd", 16)
        u_ldx = tick("u_ldx", 16)
        u_sq1 = tick("u_sq1"); u_st1 = tick("u_st1"); u_sqrt1 = tick("u_sqrt1")
        u_is1 = tick("u_is1"); u_xn1 = tick("u_xn1")
        u_T1 = tick("u_T1"); u_eT1 = tick("u_eT1")
        u_qkq = tick("u_qkq"); u_emin = tick("u_emin"); u_exp = tick("u_exp")
        u_estt = tick("u_estt"); u_csbc = tick("u_csbc", 16); u_cs6 = tick("u_cs6")
        u_rope = tick("u_rope"); u_Tq = tick("u_Tq"); u_eTq = tick("u_eTq")
        u_dots = tick("u_dots"); u_attn = tick("u_attn"); u_lepe = tick("u_lepe")
        u_epi1 = tick("u_epi1"); u_epi2 = tick("u_epi2")
        u_sq2 = tick("u_sq2"); u_st2 = tick("u_st2"); u_sqrt2 = tick("u_sqrt2")
        u_is2 = tick("u_is2"); u_xn2 = tick("u_xn2")
        u_T2 = tick("u_T2"); u_eT2 = tick("u_eT2")
        u_ffn2 = tick("u_ffn2"); u_x3 = tick("u_x3"); u_out = tick("u_out", 16)
        u_pad = tick("u_pad")
        t_kev = tick("t_kev"); u_qev = tick("u_qev")
        vself = tick("vself")
        _vc = [0]

        def vg(eng, inst):
            _vc[0] += 1
            inst.then_inc(vself.sem, 1)
            eng.wait_ge(vself.sem, _vc[0])
            return inst

        sself = tick("sself")
        _sc = [0]

        def sg(eng, inst):
            _sc[0] += 1
            inst.then_inc(sself.sem, 1)
            eng.wait_ge(sself.sem, _sc[0])
            return inst

        with ExitStack() as ast:
            asb = lambda n, s, d=F32: ast.enter_context(nc.sbuf_tensor(n, s, d))
            aps = lambda n, s, d=F32: ast.enter_context(nc.psum_tensor(n, s, d))

            kmrow_sb = asb("kmrow_sb", [H, P], BF16)
            ksf = asb("ksf", [P, H], F32)
            ksbf = asb("ksbf", [P, H], BF16)
            Wqk_k = asb("Wqk_k", [P, NK, C], BF16)
            qkbk_row = asb("qkbk_row", [1, C], BF16)
            qk_stg = asb("qk_stg", [P, 2, C], BF16)
            xa = asb("xa", [P, 4, C], F32)
            xn0b = asb("xn0b", [P, 2, C], BF16)
            xn0T = asb("xn0T", [P, 2, C], BF16)
            xn1b = asb("xn1b", [P, 4, C], BF16)
            xn1T = asb("xn1T", [P, 2, C], BF16)
            mnb = asb("mnb", [P, C], BF16)
            eb = asb("eb", [P, C], BF16)
            kpb = asb("kpb", [P, 2, C], BF16)
            krb = asb("krb", [P, 2, C], BF16)
            cab = asb("cab", [P, 2, C], BF16)
            cst = asb("cst", [P, 2, C], BF16)
            vtmp = asb("vtmp", [P, C], BF16)
            sqg = asb("sqg", [P, C], BF16)
            sm = asb("sm", [P, 8], F32)
            sn = asb("sn", [P, 8], F32)
            sm0 = asb("sm0", [P, 8], F32)
            sn0 = asb("sn0", [P, 8], F32)

            p_y = aps("p_y", [P, C], F32)
            p_k = aps("p_k", [P, C], F32)
            p_kv = aps("p_kv", [P, C], F32)
            p_ks = aps("p_ks", [P, H], F32)
            p_t = aps("p_t", [P, C], BF16)

            sst = ExitStack()
            ssb = lambda n, s, d=F32: sst.enter_context(nc.sbuf_tensor(n, s, d))
            stg = ssb("stg", [P, 2, C], F32)
            wtmp = ssb("wtmp", [P, C], BF16)
            VrS = ssb("VrS", [P, NK * R], BF16)
            CS = ssb("CS", [P, NK * R], BF16)
            VrT = ssb("VrT", [R, C], BF16)
            CT = ssb("CT", [R, C], BF16)
            idn = ssb("idn", [P, P], I32)
            tabf = ssb("tabf", [P, C], F32)
            tabi = ssb("tabi", [P, 512], I32)
            tabs2 = ssb("tabs2", [P, 512], F32)
            atf = ssb("atf", [nt, C], BF16)
            throw = ssb("throw", [1, 512], F32)
            posr = ssb("posr", [1, P], F32)
            tbr = ssb("tbr", [1, nt], F32)
            rows4 = ssb("rows4", [1, 2 * C], BF16)
            c_half = ssb("c_half", [P, 1], F32)

            with nc.Block() as blk:
                @blk.gpsimd
                def _(eng):
                    eng.iota(idn[:, :], [[1, P]], channel_multiplier=-1,
                             allow_small_or_imprecise_dtypes=True)
                    eng.iota(throw[:, 0:512], [[1, 512]], channel_multiplier=0,
                             allow_small_or_imprecise_dtypes=True)
                    eng.iota(posr[:, :], [[1, P]], channel_multiplier=0,
                             allow_small_or_imprecise_dtypes=True)
                    eng.iota(tbr[:, :], [[P, nt]], channel_multiplier=0,
                             allow_small_or_imprecise_dtypes=True)
                    eng.memset(ones1[:, :], 1.0)
                    eng.memset(ones_col[:, :], 1.0)
                    eng.memset(c_eps[:, :], EPS)
                    eng.memset(c_nhalf[:, :], -0.5)
                    eng.memset(c_inv64[:, :], 1.0 / float(np.sqrt(T)))
                    eng.memset(c_npi[:, :], -float(np.pi))
                    eng.memset(c_2pi[:, :], 2.0 * float(np.pi))
                    eng.memset(c_s1[:, :], -float(np.log(ROPE_BASE)) / 512.0)
                    eng.memset(c_half[:, :], 0.5)
            nc.all_engine_barrier()

            with nc.Block() as blk:
                @blk.sync
                def _(eng):
                    eng.dma_start(stg[:, 0, 0:NK * R],
                                  bass.AP(Vr_d.tensor, 0,
                                          [[R, P], [P * R, NK], [1, R]])
                                  ).then_inc(t_ld.sem, 16)
                    eng.dma_start(stg[:, 1, 0:NK * R],
                                  bass.AP(Cm_d.tensor, 0,
                                          [[R, P], [P * R, NK], [1, R]])
                                  ).then_inc(t_ld.sem, 16)
                    for k in range(NK):
                        if k < 2:
                            t_misc.wait(eng, 2)
                        else:
                            t_cw.wait(eng, k - 1)
                        eng.dma_start(stg[:, k % 2, :],
                                      W_d[k * P:(k + 1) * P, :]
                                      ).then_inc(t_ld.sem, 16)
                    for k in range(NK):
                        if k < 2:
                            t_cw.wait(eng, NK)
                        else:
                            t_cq.wait(eng, k - 1)
                        eng.dma_start(stg[:, k % 2, :],
                                      qkw_d[k * P:(k + 1) * P, C:2 * C]
                                      ).then_inc(t_ld.sem, 16)

                @blk.vector
                def _(eng):
                    t_ld.wait(eng, 2)
                    t_gld.wait(eng, 2)
                    eng.tensor_scalar_mul(VrS[:, :], stg[:, 0, 0:NK * R], 0.5)
                    eng.tensor_copy(CS[:, :], stg[:, 1, 0:NK * R])
                    eng.tensor_scalar_mul(bc_row[:, :], rows4[0:1, 0:C], 0.5)
                    eng.tensor_copy(lepb_row[:, :], rows4[0:1, C:2 * C])
                    vg(eng, eng.tensor_single_scalar(ident[:, :], idn[:, :],
                                                     0.0, ALU.is_equal))
                    t_misc.inc(eng.tensor_copy(tabs2[0:1, 0:1],
                                               ident[0:1, 0:1]))
                    t_gld.wait(eng, 3)
                    eng.tensor_copy(
                        qkbq_row[:, :],
                        rows4[0:1, 0:C].rearrange("p (n two) -> p n two",
                                                  two=2).transpose([0, 2, 1]))
                    t_misc.inc(eng.tensor_copy(
                        qkbk_row[:, :],
                        rows4[0:1, C:2 * C].rearrange("p (n two) -> p n two",
                                                      two=2).transpose([0, 2, 1])))

                @blk.gpsimd
                def _(eng):
                    eng.dma_start(rows4[0:1, 0:C], b_d[None, :]).then_inc(t_gld.sem, 16)
                    eng.dma_start(rows4[0:1, C:2 * C],
                                  lepb_d[None, :]).then_inc(t_gld.sem, 16)
                    t_misc.wait(eng, 1)
                    eng.dma_start(rows4[0:1, 0:2 * C],
                                  qkb_d[None, :]).then_inc(t_gld.sem, 16)
                    for k in range(NK):
                        t_ld.wait(eng, NK + 2 + k + 1)
                        t_cq.inc(eng.tensor_copy(
                            Wqk_k[:, k, :],
                            stg[:, k % 2, :].rearrange(
                                "p (n two) -> p n two", two=2
                                ).transpose([0, 2, 1])))

                @blk.tensor
                def _(eng):
                    t_misc.wait(eng, 1)
                    for a in range(NK):
                        inst = eng.transpose(p_t[0:R, a * P:(a + 1) * P],
                                             VrS[:, a * R:(a + 1) * R],
                                             ident[:, :])
                    t_tp.inc(inst)

                @blk.vector
                def _(eng):
                    t_tp.wait(eng, 1)
                    t_tv.inc(eng.tensor_copy(VrT[:, :], p_t[0:R, :]))

                @blk.tensor
                def _(eng):
                    t_tv.wait(eng, 1)
                    for a in range(NK):
                        inst = eng.transpose(p_t[0:R, a * P:(a + 1) * P],
                                             CS[:, a * R:(a + 1) * R],
                                             ident[:, :])
                    t_tp.inc(inst)

                @blk.vector
                def _(eng):
                    t_tp.wait(eng, 2)
                    t_tv.inc(eng.tensor_copy(CT[:, :], p_t[0:R, :]))

                for r2 in range(NK):
                    @blk.vector
                    def _(eng, r2=r2):
                        t_ld.wait(eng, 2 + r2 + 1)
                        if r2 > 0:
                            t_tp.wait(eng, 2 + r2)
                        t_cw.inc(eng.tensor_copy(wtmp[:, :], stg[:, r2 % 2, :]))

                    @blk.tensor
                    def _(eng, r2=r2):
                        t_cw.wait(eng, r2 + 1)
                        if r2 == 0:
                            t_tv.wait(eng, 2)
                        else:
                            t_tv.wait(eng, 2 + r2)
                        for cch in range(NK):
                            inst = eng.transpose(p_t[:, cch * P:(cch + 1) * P],
                                                 wtmp[:, cch * P:(cch + 1) * P],
                                                 ident[:, :])
                        t_tp.inc(inst)

                    @blk.scalar
                    def _(eng, r2=r2):
                        t_tp.wait(eng, 3 + r2)
                        t_tv.inc(eng.activation(
                            Wc[:, :, r2 * P:(r2 + 1) * P],
                            p_t[:, :].rearrange("p (k f) -> p k f", k=NK),
                            ACT.Copy, scale=c_half[:, :]))

                if True:
                    p_w = p_y
                    for cch in range(NK):
                        @blk.tensor
                        def _(eng, cch=cch):
                            t_tv.wait(eng, 2 + NK + cch)
                            eng.matmul(p_w[:, 0:512],
                                       VrT[:, cch * P:(cch + 1) * P],
                                       CT[:, 0:512], start=True, stop=True)
                            t_tp.inc(eng.matmul(p_w[:, 512:C],
                                                VrT[:, cch * P:(cch + 1) * P],
                                                CT[:, 512:C],
                                                start=True, stop=True))

                        @blk.vector
                        def _(eng, cch=cch):
                            t_tp.wait(eng, 3 + NK + cch)
                            t_tv.inc(eng.tensor_tensor(Wc[:, cch, :],
                                                       Wc[:, cch, :],
                                                       p_w[:, :], ALU.add))

                    @blk.scalar
                    def _(eng):
                        t_tab.inc(eng.activation(throw[0:1, 0:512],
                                                 throw[0:1, 0:512], ACT.Exp,
                                                 scale=c_s1[0:1, :]))

                    @blk.vector
                    def _(eng):
                        t_tab.wait(eng, 1)
                        t_tab.inc(eng.tensor_scalar_mul(
                            throw[0:1, 0:512], throw[0:1, 0:512],
                            1.0 / (2.0 * float(np.pi))))

                    @blk.tensor
                    def _(eng):
                        t_tab.wait(eng, 2)
                        t_tv.wait(eng, 2 + 2 * NK)
                        eng.matmul(p_w[0:nt, 0:512], tbr[:, :],
                                   throw[0:1, 0:512], start=True, stop=True)
                        t_tab.inc(eng.matmul(p_w[0:nt, 512:C], tbr[:, :],
                                             throw[0:1, 0:512],
                                             start=True, stop=True))

                    def rangered(eng, pang, rows):
                        vg(eng, eng.tensor_scalar_add(tabf[0:rows, 0:512],
                                                      pang[0:rows, 0:512], 0.25))
                        vg(eng, eng.tensor_scalar_add(tabf[0:rows, 512:C],
                                                      pang[0:rows, 512:C], 0.0))
                        for hh in range(2):
                            slh = slice(512 * hh, 512 * (hh + 1))
                            vg(eng, eng.tensor_copy(tabi[0:rows, :],
                                                    tabf[0:rows, slh]))
                            vg(eng, eng.tensor_copy(tabs2[0:rows, :],
                                                    tabi[0:rows, :]))
                            vg(eng, eng.tensor_sub(tabf[0:rows, slh],
                                                   tabf[0:rows, slh],
                                                   tabs2[0:rows, :]))
                        vg(eng, eng.tensor_single_scalar(tabs2[0:rows, :],
                                                         tabf[0:rows, 0:512],
                                                         0.5, ALU.is_ge))
                        vg(eng, eng.tensor_sub(tabf[0:rows, 0:512],
                                               tabf[0:rows, 0:512],
                                               tabs2[0:rows, :]))
                        vg(eng, eng.tensor_single_scalar(tabs2[0:rows, :],
                                                         tabf[0:rows, 512:C],
                                                         0.5, ALU.is_ge))
                        vg(eng, eng.tensor_sub(tabf[0:rows, 512:C],
                                               tabf[0:rows, 512:C],
                                               tabs2[0:rows, :]))
                        vg(eng, eng.tensor_single_scalar(tabf[0:rows, :],
                                                         tabf[0:rows, :],
                                                         -0.4999995, ALU.max))
                        vg(eng, eng.tensor_single_scalar(tabf[0:rows, :],
                                                         tabf[0:rows, :],
                                                         0.4999995, ALU.min))

                    @blk.vector
                    def _(eng):
                        t_tab.wait(eng, 3)
                        rangered(eng, p_w, nt)
                        t_tab.inc(eng.engine_nop())

                    @blk.scalar
                    def _(eng):
                        t_tab.wait(eng, 4)
                        t_tab.inc(eng.activation(atf[:, :], tabf[0:nt, :],
                                                 ACT.Sin,
                                                 scale=c_2pi[0:nt, :]))

                    @blk.sync
                    def _(eng):
                        t_tab.wait(eng, 5)
                        eng.dma_start(atab_d, atf[:, :]).then_inc(t_ab.sem, 16)

                    @blk.tensor
                    def _(eng):
                        t_tab.wait(eng, 5)
                        eng.matmul(p_w[:, 0:512], posr[:, :], throw[0:1, 0:512],
                                   start=True, stop=True)
                        t_tab.inc(eng.matmul(p_w[:, 512:C], posr[:, :],
                                             throw[0:1, 0:512],
                                             start=True, stop=True))

                    @blk.vector
                    def _(eng):
                        t_tab.wait(eng, 6)
                        rangered(eng, p_w, P)
                        t_tab.inc(eng.engine_nop())

                    @blk.scalar
                    def _(eng):
                        t_tab.wait(eng, 7)
                        t_tab.inc(eng.activation(Btab[:, :], tabf[:, :],
                                                 ACT.Sin,
                                                 scale=c_2pi[:, :]))

                @blk.vector
                def _(eng):
                    eng.memset(p_kv[:, :], 0.0)
                    eng.memset(p_ks[:, :], 0.0)

            sst.close()
            nc.all_engine_barrier()

            def A_iter(blk, i):
                jl = i + 3
                jc = i - 2
                je = i - 1
                jq = i - 2
                jk = i - 3
                jx = i - 4
                jf = i + 1

                @blk.sync
                def _(eng):
                    if 0 <= jl < nt:
                        if jl >= 4:
                            t_stx1.wait(eng, jl - 3)
                            t_xn1.wait(eng, jl - 3)
                        t_ldx.inc(eng.dma_start(xa[:, jl % 4, :],
                                                x_d[jl * P:(jl + 1) * P, :]))
                    if 0 <= jc < nt:
                        t_ab.wait(eng, 1)
                        if jc >= 2:
                            t_cs6.wait(eng, jc - 1)
                        t_csbc.inc(eng.dma_start(
                            cab[:, jc % 2, :],
                            bass.AP(atab_d.tensor, jc * C, [[0, P], [1, C]])))
                    if 0 <= i < nt:
                        t_x1.wait(eng, i + 1)
                        t_stx1.inc(eng.dma_start(x1_d[i * P:(i + 1) * P, :],
                                                 xa[:, i % 4, :]))

                @blk.vector
                def _(eng):
                    if 0 <= jq < nt:
                        t_T1.wait(eng, jq + 1)
                        t_qkk.wait(eng, jq - 1)
                        t_eT1.inc(eng.tensor_copy(xn1T[:, jq % 2, :],
                                                  p_t[:, :]))
                    if 0 <= je < nt:
                        q = je % 2
                        t_sq1.wait(eng, je + 1)
                        vg(eng, eng.tensor_mul(sm[:, 4 + q:5 + q],
                                               sm[:, 2 + q:3 + q],
                                               sm[:, 2 + q:3 + q]))
                        t_st1.inc(eng.scalar_tensor_tensor(
                            sm[:, 4 + q:5 + q], sn[:, q:q + 1], 1.0 / C,
                            sm[:, 4 + q:5 + q], ALU.mult, ALU.subtract))
                    if 0 <= je < nt:
                        q = je % 2
                        t_sqrt1.wait(eng, je + 1)
                        t_xn1.wait(eng, je - 1)
                        t_is1.inc(eng.tensor_scalar(
                            sn[:, 2 + q:3 + q], sm[:, 2 + q:3 + q],
                            sm[:, 6 + q:7 + q], -1.0, ALU.mult, ALU.mult))
                    if 0 <= i < nt:
                        t_ffn.wait(eng, i + 1)
                        t_xn0.wait(eng, i + 1)
                        t_x1.inc(eng.tensor_tensor(xa[:, i % 4, :],
                                                   xa[:, i % 4, :], p_y[:, :],
                                                   ALU.add))
                    if 0 <= i < nt:
                        q = i % 2
                        t_st1.wait(eng, i - 1)
                        vg(eng, eng.tensor_reduce(sm[:, q:q + 1],
                                                  xa[:, i % 4, :],
                                                  AX.X, ALU.add))
                        vg(eng, eng.tensor_scalar_mul(sm[:, 2 + q:3 + q],
                                                      sm[:, q:q + 1], 1.0 / C))
                    if 0 <= jf < nt:
                        t_T0.wait(eng, jf + 1)
                        t_ffn.wait(eng, jf - 1)
                        t_eT0.inc(eng.tensor_copy(xn0T[:, jf % 2, :],
                                                  p_t[:, :]))
                    if 0 <= jx < nt:
                        t_kev.wait(eng, jx + 1)
                        t_exp.wait(eng, jx - 1)
                        t_emin.inc(eng.tensor_single_scalar(
                            mnb[:, :], kpb[:, jx % 2, :], 0.0, ALU.min))
                    if 0 <= jx < nt:
                        t_exp.wait(eng, jx + 1)
                        t_estt.inc(eng.scalar_tensor_tensor(
                            kpb[:, jx % 2, :], kpb[:, jx % 2, :], 0.0,
                            eb[:, :], ALU.max, ALU.add))
                    if 0 <= jx < nt:
                        t_cs6.wait(eng, jx + 1)
                        t_kv.wait(eng, jx - 1)
                        cs = cst[:, jx % 2, :]
                        kp = kpb[:, jx % 2, :]
                        eng.tensor_mul(vtmp[:, 0:512], cs[:, 0:512], kp[:, 0:512])
                        eng.tensor_mul(vtmp[:, 512:C], cs[:, 512:C], kp[:, 512:C])
                        eng.tensor_sub(krb[:, jx % 2, 0:512], vtmp[:, 0:512],
                                       vtmp[:, 512:C])
                        eng.tensor_mul(vtmp[:, 0:512], cs[:, 0:512], kp[:, 512:C])
                        eng.tensor_mul(vtmp[:, 512:C], cs[:, 512:C], kp[:, 0:512])
                        t_rope.inc(eng.tensor_add(krb[:, jx % 2, 512:C],
                                                  vtmp[:, 0:512],
                                                  vtmp[:, 512:C]))
                    if 0 <= jk < nt:
                        t_csbc.wait(eng, jk + 1)
                        t_rope.wait(eng, jk - 1)
                        ca = cab[:, jk % 2, :]
                        cd = cst[:, jk % 2, :]
                        eng.tensor_mul(vtmp[:, 0:512], ca[:, 0:512], Btab[:, 0:512])
                        eng.tensor_mul(vtmp[:, 512:C], ca[:, 512:C], Btab[:, 512:C])
                        eng.tensor_sub(cd[:, 0:512], vtmp[:, 0:512],
                                       vtmp[:, 512:C])
                        eng.tensor_mul(vtmp[:, 0:512], ca[:, 512:C], Btab[:, 0:512])
                        eng.tensor_mul(vtmp[:, 512:C], ca[:, 0:512], Btab[:, 512:C])
                        t_cs6.inc(eng.tensor_add(cd[:, 512:C], vtmp[:, 0:512],
                                                 vtmp[:, 512:C]))
                    if 0 <= jl < nt:
                        q = jl % 2
                        t_ldx.wait(eng, jl + 1)
                        t_xn0.wait(eng, jl - 1)
                        vg(eng, eng.tensor_reduce(sm0[:, q:q + 1],
                                                  xa[:, jl % 4, :],
                                                  AX.X, ALU.add))
                        vg(eng, eng.tensor_scalar_mul(sm0[:, 2 + q:3 + q],
                                                      sm0[:, q:q + 1], 1.0 / C))
                        t_sq0.wait(eng, jl + 1)
                        vg(eng, eng.tensor_mul(sm0[:, 4 + q:5 + q],
                                               sm0[:, 2 + q:3 + q],
                                               sm0[:, 2 + q:3 + q]))
                        t_st0.inc(eng.scalar_tensor_tensor(
                            sm0[:, 4 + q:5 + q], sn0[:, q:q + 1], 1.0 / C,
                            sm0[:, 4 + q:5 + q], ALU.mult, ALU.subtract))
                    if 0 <= jl < nt:
                        q = jl % 2
                        t_sqrt0.wait(eng, jl + 1)
                        t_is0.inc(eng.tensor_scalar(
                            sn0[:, 2 + q:3 + q], sm0[:, 2 + q:3 + q],
                            sm0[:, 6 + q:7 + q], -1.0, ALU.mult, ALU.mult))

                @blk.scalar
                def _(eng):
                    if 0 <= je < nt:
                        q = je % 2
                        t_x1.wait(eng, je + 1)
                        t_st1.wait(eng, je - 1)
                        t_sq1.inc(eng.activation(sqg[:, :], xa[:, je % 4, :],
                                                 ACT.Square,
                                                 accum_out=sn[:, q:q + 1]))
                    if 0 <= je < nt:
                        q = je % 2
                        t_st1.wait(eng, je + 1)
                        t_is1.wait(eng, je - 1)
                        sg(eng, eng.activation(sn[:, 6 + q:7 + q],
                                               sm[:, 4 + q:5 + q],
                                               ACT.Ln, bias=c_eps[:, :]))
                        t_sqrt1.inc(eng.activation(
                            sm[:, 6 + q:7 + q], sn[:, 6 + q:7 + q],
                            ACT.Exp, scale=c_nhalf[:, :]))
                        t_sqrt1.wait(eng, je + 1)
                    if 0 <= je < nt:
                        q = je % 2
                        t_is1.wait(eng, je + 1)
                        t_kv.wait(eng, je - 3)
                        t_xn1.inc(eng.activation(xn1b[:, je % 4, :],
                                                 xa[:, je % 4, :],
                                                 ACT.Identity,
                                                 bias=sn[:, 2 + q:3 + q],
                                                 scale=sm[:, 6 + q:7 + q]))
                    if 0 <= jk < nt:
                        t_qkk.wait(eng, jk + 1)
                        t_estt.wait(eng, jk - 1)
                        t_ks.wait(eng, jk - 1)
                        t_kev.inc(eng.activation(kpb[:, jk % 2, :], p_k[:, :],
                                                 ACT.Copy))
                    if 0 <= jx < nt:
                        t_emin.wait(eng, jx + 1)
                        t_estt.wait(eng, jx - 1)
                        t_exp.inc(eng.activation(eb[:, :], mnb[:, :], ACT.Exp))
                    if 0 <= jl < nt:
                        q = jl % 2
                        t_ldx.wait(eng, jl + 1)
                        t_st0.wait(eng, jl - 1)
                        t_sq0.inc(eng.activation(sqg[:, :], xa[:, jl % 4, :],
                                                 ACT.Square,
                                                 accum_out=sn0[:, q:q + 1]))
                    if 0 <= jl < nt:
                        q = jl % 2
                        t_st0.wait(eng, jl + 1)
                        t_is0.wait(eng, jl - 1)
                        sg(eng, eng.activation(sn0[:, 6 + q:7 + q],
                                               sm0[:, 4 + q:5 + q],
                                               ACT.Ln, bias=c_eps[:, :]))
                        t_sqrt0.inc(eng.activation(
                            sm0[:, 6 + q:7 + q], sn0[:, 6 + q:7 + q],
                            ACT.Exp, scale=c_nhalf[:, :]))
                        t_sqrt0.wait(eng, jl + 1)
                    if 0 <= jl < nt:
                        q = jl % 2
                        t_is0.wait(eng, jl + 1)
                        t_T0.wait(eng, jl - 1)
                        t_xn0.inc(eng.activation(xn0b[:, q, :], xa[:, jl % 4, :],
                                                 ACT.Identity,
                                                 bias=sn0[:, 2 + q:3 + q],
                                                 scale=sm0[:, 6 + q:7 + q]))

                @blk.tensor
                def _(eng):
                    if 0 <= i < nt:
                        t_eT0.wait(eng, i + 1)
                        t_x1.wait(eng, i)
                        for h2 in range(2):
                            sl = slice(512 * h2, 512 * (h2 + 1))
                            for k in range(NK):
                                eng.matmul(p_y[:, sl],
                                           xn0T[:, i % 2, k * P:(k + 1) * P],
                                           Wc[:, k, sl],
                                           start=(k == 0), stop=False)
                            inst = eng.matmul(p_y[:, sl], ones1[:, :],
                                              bc_row[:, sl],
                                              start=False, stop=True)
                        t_ffn.inc(inst)
                    if 0 <= jf < nt:
                        t_xn0.wait(eng, jf + 1)
                        t_eT1.wait(eng, jf - 2)
                        for k in range(NK):
                            inst = eng.transpose(p_t[:, k * P:(k + 1) * P],
                                                 xn0b[:, jf % 2,
                                                      k * P:(k + 1) * P],
                                                 ident[:, :])
                        t_T0.inc(inst)
                    if 0 <= jq < nt:
                        t_eT1.wait(eng, jq + 1)
                        t_kev.wait(eng, jq)
                        for h2 in range(2):
                            sl = slice(512 * h2, 512 * (h2 + 1))
                            for k in range(NK):
                                eng.matmul(p_k[:, sl],
                                           xn1T[:, jq % 2, k * P:(k + 1) * P],
                                           Wqk_k[:, k, sl],
                                           start=(k == 0), stop=False)
                            inst = eng.matmul(p_k[:, sl], ones1[:, :],
                                              qkbk_row[:, sl],
                                              start=False, stop=True)
                        t_qkk.inc(inst)
                    if 0 <= je < nt:
                        t_xn1.wait(eng, je + 1)
                        t_eT0.wait(eng, min(je + 3, nt))
                        for k in range(NK):
                            inst = eng.transpose(p_t[:, k * P:(k + 1) * P],
                                                 xn1b[:, je % 4,
                                                      k * P:(k + 1) * P],
                                                 ident[:, :])
                        t_T1.inc(inst)
                    if 0 <= jx < nt:
                        t_rope.wait(eng, jx + 1)
                        t_xn1.wait(eng, jx + 1)
                        for h in range(H):
                            inst = eng.matmul(p_kv[:, h * P:(h + 1) * P],
                                              krb[:, jx % 2, h * P:(h + 1) * P],
                                              xn1b[:, jx % 4, h * P:(h + 1) * P],
                                              start=False, stop=True,
                                              skip_group_check=True)
                        t_kv.inc(inst)
                    if 0 <= jx < nt:
                        t_estt.wait(eng, jx + 1)
                        for k in range(NK):
                            inst = eng.matmul(p_ks[:, k:k + 1],
                                              kpb[:, jx % 2, k * P:(k + 1) * P],
                                              ones_col[:, :],
                                              start=False, stop=True,
                                              skip_group_check=True)
                        t_ks.inc(inst)

            def emit_deferred_loads(blk):
                @blk.gpsimd
                def _(eng):
                    eng.dma_start(qk_stg[:, 0, :],
                                  qkw_d[0:P, 0:C]).then_inc(t_lq.sem, 16)
                    eng.dma_start(qk_stg[:, 1, :],
                                  qkw_d[P:2 * P, 0:C]).then_inc(t_lq.sem, 16)
                    for ii in range(NK):
                        t_lq.wait(eng, ii + 1)
                        eng.tensor_copy(
                            Wqk_q[:, ii, :],
                            qk_stg[:, ii % 2, :].rearrange(
                                "p (n two) -> p n two", two=2
                                ).transpose([0, 2, 1])).then_inc(t_wb.sem, 1)
                        if ii + 2 < NK:
                            t_wb.wait(eng, ii + 1)
                            eng.dma_start(qk_stg[:, ii % 2, :],
                                          qkw_d[(ii + 2) * P:(ii + 3) * P, 0:C]
                                          ).then_inc(t_lq.sem, 16)
                    for t3 in range(3):
                        for k in range(NK):
                            eng.dma_start(Wlep[:, t3, k, :],
                                          lepw_d[t3, k * P:(k + 1) * P, :]
                                          ).then_inc(t_lq.sem, 16)

            if serial:
                with nc.Block() as blk:
                    emit_deferred_loads(blk)
                nc.all_engine_barrier()
                for it in range(-3, nt + 4):
                    with nc.Block() as blk:
                        A_iter(blk, it)
                    nc.all_engine_barrier()
                blk = nc.Block().__enter__()
            else:
                blk = nc.Block().__enter__()
                emit_deferred_loads(blk)
                for it in range(-3, nt + 4):
                    A_iter(blk, it)
            if True:

                @blk.scalar
                def _(eng):
                    t_kv.wait(eng, nt)
                    eng.activation(kv_sb[:, 0:512], p_kv[:, 0:512], ACT.Copy,
                                   scale=c_inv64[:, :])
                    t_aend.inc(eng.activation(kv_sb[:, 512:C], p_kv[:, 512:C],
                                              ACT.Copy, scale=c_inv64[:, :]))

                @blk.vector
                def _(eng):
                    t_ks.wait(eng, nt)
                    vg(eng, eng.tensor_scalar_mul(ksf[:, :], p_ks[:, :], 1.0 / T))
                    t_aend.inc(eng.tensor_copy(ksbf[:, :], ksf[:, :]))

                @blk.tensor
                def _(eng):
                    t_aend.wait(eng, 2)
                    t_eT1.wait(eng, nt)
                    t_aend.inc(eng.transpose(p_t[0:H, 0:P], ksbf[:, :],
                                             ident[:, :]))

                @blk.scalar
                def _(eng):
                    t_aend.wait(eng, 3)
                    t_aend.inc(eng.activation(kmrow_sb[:, :], p_t[0:H, 0:P],
                                              ACT.Copy))

                @blk.sync
                def _(eng):
                    t_aend.wait(eng, 4)
                    eng.dma_start(kmrow_d.rearrange("one (p f) -> (one p) f",
                                                     p=H),
                                  kmrow_sb[:, :]).then_inc(t_kmbd.sem, 16)
                    t_kmbd.wait(eng, 1)
                    eng.dma_start(kmb[:, :],
                                  bass.AP(kmrow_d.tensor, 0, [[0, P], [1, C]])
                                  ).then_inc(t_kmbd.sem, 16)
                    t_kmbd.wait(eng, 2)
            blk.__exit__(None, None, None)

            nc.all_engine_barrier()

        with ExitStack() as bst:
            bsb = lambda n, s, d=F32: bst.enter_context(nc.sbuf_tensor(n, s, d))
            bps = lambda n, s, d=F32: bst.enter_context(nc.psum_tensor(n, s, d))
            xb = bsb("xb", [P, 4, C], F32)
            xn1c = bsb("xn1c", [P, 2, C], BF16)
            xnT_all = bsb("xnT_all", [P, NK, nt * P + 2], BF16)
            qpb = bsb("qpb", [P, 2, C], BF16)
            mnb2 = bsb("mnb2", [P, C], BF16)
            eb2 = bsb("eb2", [P, C], BF16)
            cab2 = bsb("cab2", [P, 2, C], BF16)
            cst2 = bsb("cst2", [P, 1, C], BF16)
            vtmp2 = bsb("vtmp2", [P, C], BF16)
            qrb = bsb("qrb", [P, C], BF16)
            qrT = bsb("qrT", [P, 2, C], BF16)
            xn2b = bsb("xn2b", [P, 2, C], BF16)
            sqg2 = bsb("sqg2", [P, C], BF16)
            sm2 = bsb("sm2", [P, 8], F32)
            sn2 = bsb("sn2", [P, 8], F32)
            sm3 = bsb("sm3", [P, 8], F32)
            sn3 = bsb("sn3", [P, 8], F32)
            dsg = bsb("dsg", [P, 16], F32)
            zb = bsb("zb", [P, 2 * H], F32)
            xn2T = bsb("xn2T", [P, C], BF16)

            p_qa = bps("p_qa", [P, C], F32)
            p_lep = bps("p_lep", [P, C], F32)
            p_y2 = bps("p_y2", [P, C], F32)
            p_t2 = bps("p_t2", [P, C], BF16)
            p_tq = bps("p_tq", [P, C], BF16)

            def B_iter(blk, i):
                jp, j, ja, jb, jc = i + 1, i, i - 1, i - 2, i - 3

                @blk.sync
                def _(eng):
                    if 0 <= jc < nt:
                        u_x3.wait(eng, jc + 1)
                        u_out.inc(eng.dma_start(out_d[jc * P:(jc + 1) * P, :],
                                                xb[:, jc % 4, :]))
                    if 0 <= jp < nt:
                        if jp >= 4:
                            u_out.wait(eng, jp - 3)
                        u_ldx.inc(eng.dma_start(xb[:, jp % 4, :],
                                                x1_d[jp * P:(jp + 1) * P, :]))
                    if 0 <= j < nt:
                        if j >= 2:
                            u_cs6.wait(eng, j - 1)
                        u_csbc.inc(eng.dma_start(
                            cab2[:, j % 2, :],
                            bass.AP(atab_d.tensor, j * C, [[0, P], [1, C]])))

                @blk.vector
                def _(eng):
                    if i == -1:
                        eng.memset(xnT_all[:, :, 0:1], 0.0)
                        u_pad.inc(eng.memset(
                            xnT_all[:, :, nt * P + 1:nt * P + 2], 0.0))
                    if 0 <= ja < nt:
                        u_qev.wait(eng, ja + 1)
                        u_exp.wait(eng, ja)
                        u_emin.inc(eng.tensor_single_scalar(
                            mnb2[:, :], qpb[:, ja % 2, :], 0.0, ALU.min))
                        u_exp.wait(eng, ja + 1)
                        u_estt.inc(eng.scalar_tensor_tensor(
                            qpb[:, ja % 2, :], qpb[:, ja % 2, :], 0.0,
                            eb2[:, :], ALU.max, ALU.add))
                    if 0 <= ja < nt:
                        u_csbc.wait(eng, ja + 1)
                        ca = cab2[:, ja % 2, :]
                        cd = cst2[:, 0, :]
                        eng.tensor_mul(vtmp2[:, 0:512], ca[:, 0:512],
                                       Btab[:, 0:512])
                        eng.tensor_mul(vtmp2[:, 512:C], ca[:, 512:C],
                                       Btab[:, 512:C])
                        eng.tensor_sub(cd[:, 0:512], vtmp2[:, 0:512],
                                       vtmp2[:, 512:C])
                        eng.tensor_mul(vtmp2[:, 0:512], ca[:, 512:C],
                                       Btab[:, 0:512])
                        eng.tensor_mul(vtmp2[:, 512:C], ca[:, 0:512],
                                       Btab[:, 512:C])
                        u_cs6.inc(eng.tensor_add(cd[:, 512:C], vtmp2[:, 0:512],
                                                 vtmp2[:, 512:C]))
                    if 0 <= ja < nt:
                        q8 = 8 * (ja % 2)
                        eng.tensor_mul(vtmp2[:, :], qpb[:, ja % 2, :], kmb[:, :])
                        vg(eng, eng.tensor_reduce(dsg[:, :],
                                                  vtmp2[:, :].rearrange(
                                                      "p (s w) -> p s w", s=16),
                                                  AX.X, ALU.add))
                        vg(eng, eng.tensor_add(zb[:, q8:q8 + 8], dsg[:, 0:8],
                                               dsg[:, 8:16]))
                        vg(eng, eng.tensor_scalar_add(zb[:, q8:q8 + 8],
                                                      zb[:, q8:q8 + 8], 1e-6))
                        vg(eng, eng.reciprocal(zb[:, q8:q8 + 8],
                                               zb[:, q8:q8 + 8]))
                    if 0 <= ja < nt:
                        u_Tq.wait(eng, ja)
                        q8 = 8 * (ja % 2)
                        cs = cst2[:, 0, :]
                        kp = qpb[:, ja % 2, :]
                        eng.tensor_mul(vtmp2[:, 0:512], cs[:, 0:512], kp[:, 0:512])
                        eng.tensor_mul(vtmp2[:, 512:C], cs[:, 512:C], kp[:, 512:C])
                        eng.tensor_sub(qrb[:, 0:512], vtmp2[:, 0:512],
                                       vtmp2[:, 512:C])
                        eng.tensor_mul(vtmp2[:, 0:512], cs[:, 0:512], kp[:, 512:C])
                        eng.tensor_mul(vtmp2[:, 512:C], cs[:, 512:C], kp[:, 0:512])
                        vg(eng, eng.tensor_add(qrb[:, 512:C], vtmp2[:, 0:512],
                                               vtmp2[:, 512:C]))
                        for h in range(H):
                            inst = eng.tensor_scalar_mul(
                                qrb[:, h * P:(h + 1) * P],
                                qrb[:, h * P:(h + 1) * P],
                                zb[:, q8 + h:q8 + h + 1])
                        u_rope.inc(inst)
                    if 0 <= jb < nt:
                        u_attn.wait(eng, jb + 1)
                        u_epi1.inc(eng.tensor_tensor(xb[:, jb % 4, :],
                                                     xb[:, jb % 4, :],
                                                     p_lep[:, :], ALU.add))
                    if 0 <= jc < nt:
                        u_ffn2.wait(eng, jc + 1)
                        u_x3.inc(eng.tensor_tensor(xb[:, jc % 4, :],
                                                   xb[:, jc % 4, :],
                                                   p_y2[:, :], ALU.add))
                    if 0 <= jp < nt:
                        q = jp % 2
                        u_ldx.wait(eng, jp + 1)
                        vg(eng, eng.tensor_reduce(sm2[:, q:q + 1],
                                                  xb[:, jp % 4, :],
                                                  AX.X, ALU.add))
                        vg(eng, eng.tensor_scalar_mul(sm2[:, 2 + q:3 + q],
                                                      sm2[:, q:q + 1], 1.0 / C))
                        u_sq1.wait(eng, jp + 1)
                        vg(eng, eng.tensor_mul(sm2[:, 4 + q:5 + q],
                                               sm2[:, 2 + q:3 + q],
                                               sm2[:, 2 + q:3 + q]))
                        u_st1.inc(eng.scalar_tensor_tensor(
                            sm2[:, 4 + q:5 + q], sn2[:, q:q + 1], 1.0 / C,
                            sm2[:, 4 + q:5 + q], ALU.mult, ALU.subtract))
                        u_sqrt1.wait(eng, jp + 1)
                        u_is1.inc(eng.tensor_scalar(
                            sn2[:, 2 + q:3 + q], sm2[:, 2 + q:3 + q],
                            sm2[:, 6 + q:7 + q], -1.0, ALU.mult, ALU.mult))
                    if 0 <= jb < nt:
                        q = jb % 2
                        vg(eng, eng.tensor_reduce(sm3[:, q:q + 1],
                                                  xb[:, jb % 4, :],
                                                  AX.X, ALU.add))
                        vg(eng, eng.tensor_scalar_mul(sm3[:, 2 + q:3 + q],
                                                      sm3[:, q:q + 1], 1.0 / C))
                        u_sq2.wait(eng, jb + 1)
                        vg(eng, eng.tensor_mul(sm3[:, 4 + q:5 + q],
                                               sm3[:, 2 + q:3 + q],
                                               sm3[:, 2 + q:3 + q]))
                        u_st2.inc(eng.scalar_tensor_tensor(
                            sm3[:, 4 + q:5 + q], sn3[:, q:q + 1], 1.0 / C,
                            sm3[:, 4 + q:5 + q], ALU.mult, ALU.subtract))
                        u_sqrt2.wait(eng, jb + 1)
                        u_is2.inc(eng.tensor_scalar(
                            sn3[:, 2 + q:3 + q], sm3[:, 2 + q:3 + q],
                            sm3[:, 6 + q:7 + q], -1.0, ALU.mult, ALU.mult))

                @blk.scalar
                def _(eng):
                    if 0 <= j < nt:
                        u_T1.wait(eng, j + 1)
                        u_eT1.inc(eng.activation(
                            xnT_all[:, :, 1 + j * P:1 + (j + 1) * P],
                            p_t2[:, :].rearrange("p (k f) -> p k f", k=NK),
                            ACT.Copy))
                    if 0 <= ja < nt:
                        u_qkq.wait(eng, ja + 1)
                        u_rope.wait(eng, ja - 1)
                        u_qev.inc(eng.activation(qpb[:, ja % 2, :], p_qa[:, :],
                                                 ACT.Copy))
                    if 0 <= ja < nt:
                        u_emin.wait(eng, ja + 1)
                        u_estt.wait(eng, ja)
                        u_exp.inc(eng.activation(eb2[:, :], mnb2[:, :], ACT.Exp))
                    if 0 <= jc < nt:
                        u_T2.wait(eng, jc + 1)
                        u_ffn2.wait(eng, jc)
                        u_eT2.inc(eng.activation(xn2T[:, :], p_t2[:, :],
                                                 ACT.Copy))
                    if 0 <= jp < nt:
                        q = jp % 2
                        u_ldx.wait(eng, jp + 1)
                        u_st1.wait(eng, jp - 1)
                        u_sq1.inc(eng.activation(sqg2[:, :], xb[:, jp % 4, :],
                                                 ACT.Square,
                                                 accum_out=sn2[:, q:q + 1]))
                        u_st1.wait(eng, jp + 1)
                        u_is1.wait(eng, jp - 1)
                        sg(eng, eng.activation(sn2[:, 6 + q:7 + q],
                                               sm2[:, 4 + q:5 + q], ACT.Ln,
                                               bias=c_eps[:, :]))
                        u_sqrt1.inc(eng.activation(
                            sm2[:, 6 + q:7 + q], sn2[:, 6 + q:7 + q],
                            ACT.Exp, scale=c_nhalf[:, :]))
                        u_sqrt1.wait(eng, jp + 1)
                        u_is1.wait(eng, jp + 1)
                        u_T1.wait(eng, jp - 1)
                        u_xn1.inc(eng.activation(xn1c[:, q, :], xb[:, jp % 4, :],
                                                 ACT.Identity,
                                                 bias=sn2[:, 2 + q:3 + q],
                                                 scale=sm2[:, 6 + q:7 + q]))
                    if 0 <= jb < nt:
                        q = jb % 2
                        u_epi1.wait(eng, jb + 1)
                        u_st2.wait(eng, jb - 1)
                        u_sq2.inc(eng.activation(sqg2[:, :], xb[:, jb % 4, :],
                                                 ACT.Square,
                                                 accum_out=sn3[:, q:q + 1]))
                        u_st2.wait(eng, jb + 1)
                        u_is2.wait(eng, jb - 1)
                        sg(eng, eng.activation(sn3[:, 6 + q:7 + q],
                                               sm3[:, 4 + q:5 + q], ACT.Ln,
                                               bias=c_eps[:, :]))
                        u_sqrt2.inc(eng.activation(
                            sm3[:, 6 + q:7 + q], sn3[:, 6 + q:7 + q],
                            ACT.Exp, scale=c_nhalf[:, :]))
                        u_sqrt2.wait(eng, jb + 1)
                        u_is2.wait(eng, jb + 1)
                        u_T2.wait(eng, jb)
                        u_xn2.inc(eng.activation(xn2b[:, jb % 2, :],
                                                 xb[:, jb % 4, :],
                                                 ACT.Identity,
                                                 bias=sn3[:, 2 + q:3 + q],
                                                 scale=sm3[:, 6 + q:7 + q]))
                    if 0 <= ja < nt:
                        u_Tq.wait(eng, ja + 1)
                        u_attn.wait(eng, ja - 1)
                        u_eTq.inc(eng.activation(qrT[:, ja % 2, :], p_tq[:, :],
                                                 ACT.Copy))

                @blk.tensor
                def _(eng):
                    if 0 <= j < nt:
                        u_xn1.wait(eng, j + 1)
                        u_eT2.wait(eng, j - 3)
                        for k in range(NK):
                            inst = eng.transpose(p_t2[:, k * P:(k + 1) * P],
                                                 xn1c[:, j % 2,
                                                      k * P:(k + 1) * P],
                                                 ident[:, :])
                        u_T1.inc(inst)
                    if 0 <= jb < nt:
                        if jb == 0:
                            t_lq.wait(eng, NK + 3 * NK)
                        u_eT1.wait(eng, min(jb + 2, nt))
                        u_pad.wait(eng, 1)
                        u_epi1.wait(eng, jb)
                        for h2 in range(2):
                            sl = slice(512 * h2, 512 * (h2 + 1))
                            first = True
                            for tap in range(3):
                                for k in range(NK):
                                    eng.matmul(
                                        p_lep[:, sl],
                                        xnT_all[:, k,
                                                jb * P + tap:jb * P + tap + P],
                                        Wlep[:, tap, k, sl],
                                        start=first, stop=False)
                                    first = False
                            inst = eng.matmul(p_lep[:, sl], ones1[:, :],
                                              lepb_row[:, sl],
                                              start=False, stop=True)
                        u_lepe.inc(inst)
                    if 0 <= jc < nt:
                        u_xn2.wait(eng, jc + 1)
                        u_eT1.wait(eng, min(jc + 4, nt))
                        for k in range(NK):
                            inst = eng.transpose(p_t2[:, k * P:(k + 1) * P],
                                                 xn2b[:, jc % 2,
                                                      k * P:(k + 1) * P],
                                                 ident[:, :])
                        u_T2.inc(inst)
                    if 0 <= jb < nt:
                        u_eTq.wait(eng, jb + 1)
                        u_lepe.wait(eng, jb + 1)
                        for h in range(H):
                            inst = eng.matmul(p_lep[:, h * P:(h + 1) * P],
                                              qrT[:, jb % 2, h * P:(h + 1) * P],
                                              kv_sb[:, h * P:(h + 1) * P],
                                              start=False, stop=True,
                                              skip_group_check=True)
                        u_attn.inc(inst)
                    if 0 <= jc < nt:
                        u_eT2.wait(eng, jc + 1)
                        u_x3.wait(eng, jc)
                        for h2 in range(2):
                            sl = slice(512 * h2, 512 * (h2 + 1))
                            for k in range(NK):
                                eng.matmul(p_y2[:, sl],
                                           xn2T[:, k * P:(k + 1) * P],
                                           Wc[:, k, sl],
                                           start=(k == 0), stop=False)
                            inst = eng.matmul(p_y2[:, sl], ones1[:, :],
                                              bc_row[:, sl],
                                              start=False, stop=True)
                        u_ffn2.inc(inst)
                    if 0 <= j < nt:
                        if j == 0:
                            t_wb.wait(eng, NK)
                        u_eT1.wait(eng, j + 1)
                        u_qev.wait(eng, j)
                        for h2 in range(2):
                            sl = slice(512 * h2, 512 * (h2 + 1))
                            for k in range(NK):
                                eng.matmul(p_qa[:, sl],
                                           xnT_all[:, k,
                                                   1 + j * P:1 + (j + 1) * P],
                                           Wqk_q[:, k, sl],
                                           start=(k == 0), stop=False)
                            inst = eng.matmul(p_qa[:, sl], ones1[:, :],
                                              qkbq_row[:, sl],
                                              start=False, stop=True)
                        u_qkq.inc(inst)
                    if 0 <= ja < nt:
                        u_rope.wait(eng, ja + 1)
                        u_eTq.wait(eng, ja)
                        for k in range(NK):
                            inst = eng.transpose(p_tq[:, k * P:(k + 1) * P],
                                                 qrb[:, k * P:(k + 1) * P],
                                                 ident[:, :])
                        u_Tq.inc(inst)

            if serial:
                for it in range(-1, nt + 3):
                    with nc.Block() as blk:
                        B_iter(blk, it)
                    nc.all_engine_barrier()
                blk = nc.Block().__enter__()
            else:
                blk = nc.Block().__enter__()
                for it in range(-1, nt + 3):
                    B_iter(blk, it)
            if True:
                @blk.sync
                def _(eng):
                    u_out.wait(eng, nt)
            blk.__exit__(None, None, None)

    return nc


_NC_CACHE = {}


def kernel(**inputs):
    T = inputs["L0_attn_x"].shape[1]
    B = inputs["L0_attn_x"].shape[0]
    key = (T,)
    if key not in _NC_CACHE:
        _NC_CACHE[key] = build_nc(T=T)
    nc = _NC_CACHE[key]
    names = ["L0_attn_qk_w", "L0_attn_qk_b", "L0_attn_lepe_w", "L0_attn_lepe_b",
             "L0_ffn_W", "L0_ffn_b", "L0_ffn_V_r", "L0_ffn_C"]
    shared = {n: np.ascontiguousarray(np.asarray(inputs[n], dtype=np.float32))
              for n in names}
    x = np.ascontiguousarray(np.asarray(inputs["L0_attn_x"], dtype=np.float32))
    in_maps = [dict(shared, L0_attn_x=x[i]) for i in range(B)]
    res = run_bass_kernel_spmd(nc, in_maps, core_ids=list(range(B)))
    return np.stack([np.asarray(r["out"], dtype=np.float32)
                     for r in res.results], axis=0)


# version 18 (final)
# speedup vs baseline: 1.2084x; 1.0151x over previous
import numpy as np
import ml_dtypes
from contextlib import ExitStack

import concourse.bass as bass
import concourse.mybir as mybir
from concourse.bass_utils import run_bass_kernel_spmd

F32 = mybir.dt.float32
BF16 = mybir.dt.bfloat16
I32 = mybir.dt.int32
ALU = mybir.AluOpType
ACT = mybir.ActivationFunctionType
AX = mybir.AxisListType

C = 1024
H = 8
NK = C // 128
R = 32
EPS = 1e-5
ROPE_BASE = 10000.0
P = 128


class Tick:
    def __init__(self, stack, nc, name, step=1):
        self.sem = stack.enter_context(nc.semaphore(name))
        self.step = step

    def inc(self, inst):
        return inst.then_inc(self.sem, self.step)

    def wait(self, eng, k):
        if k > 0:
            eng.wait_ge(self.sem, k * self.step)


def build_nc(T=4096, dbg=False, serial=False):
    nt = T // P
    nc = bass.Bass("TRN2", target_bir_lowering=False,
                   detect_race_conditions=False)

    x_d = nc.dram_tensor("L0_attn_x", [T, C], F32, kind="ExternalInput").ap()
    qkw_d = nc.dram_tensor("L0_attn_qk_w", [C, 2 * C], F32, kind="ExternalInput").ap()
    qkb_d = nc.dram_tensor("L0_attn_qk_b", [2 * C], F32, kind="ExternalInput").ap()
    lepw_d = nc.dram_tensor("L0_attn_lepe_w", [3, C, C], F32, kind="ExternalInput").ap()
    lepb_d = nc.dram_tensor("L0_attn_lepe_b", [C], F32, kind="ExternalInput").ap()
    W_d = nc.dram_tensor("L0_ffn_W", [C, C], F32, kind="ExternalInput").ap()
    b_d = nc.dram_tensor("L0_ffn_b", [C], F32, kind="ExternalInput").ap()
    Vr_d = nc.dram_tensor("L0_ffn_V_r", [C, R], F32, kind="ExternalInput").ap()
    Cm_d = nc.dram_tensor("L0_ffn_C", [C, R], F32, kind="ExternalInput").ap()
    out_d = nc.dram_tensor("out", [T, C], F32, kind="ExternalOutput").ap()
    x1_d = nc.dram_tensor("x1buf", [T, C], F32,
                          kind="ExternalOutput" if dbg else "Internal").ap()
    if dbg:
        dkv_d = nc.dram_tensor("dbg_kv", [P, C], F32, kind="ExternalOutput").ap()
        dkm_d = nc.dram_tensor("dbg_kmb", [P, C], F32, kind="ExternalOutput").ap()
        dbt_d = nc.dram_tensor("dbg_btab", [P, C], F32, kind="ExternalOutput").ap()
    atab_d = nc.dram_tensor("atab_d", [nt, C], BF16,
                            kind="ExternalOutput" if dbg else "Internal").ap()
    kmrow_d = nc.dram_tensor("kmrow_d", [1, C], BF16, kind="Internal").ap()

    st = ExitStack()
    with st:
        sb = lambda n, s, d=F32: st.enter_context(nc.sbuf_tensor(n, s, d))
        tick = lambda n, step=1: Tick(st, nc, n, step)

        Wqk_q = sb("Wqk_q", [P, NK, C], BF16)
        Wlep = sb("Wlep", [P, 3, NK, C], BF16)
        Wc = sb("Wc", [P, NK, C], BF16)
        kv_sb = sb("kv_sb", [P, C], BF16)
        kmb = sb("kmb", [P, C], BF16)
        Btab = sb("Btab", [P, C], BF16)
        bc_row = sb("bc_row", [1, C], BF16)
        qkbq_row = sb("qkbq_row", [1, C], BF16)
        lepb_row = sb("lepb_row", [1, C], BF16)
        ones1 = sb("ones1", [1, P], BF16)
        ones_col = sb("ones_col", [P, 1], BF16)
        ident = sb("ident", [P, P], BF16)
        c_eps = sb("c_eps", [P, 1], F32)
        c_nhalf = sb("c_nhalf", [P, 1], F32)
        c_inv64 = sb("c_inv64", [P, 1], F32)
        c_npi = sb("c_npi", [P, 1], F32)
        c_2pi = sb("c_2pi", [P, 1], F32)
        c_s1 = sb("c_s1", [P, 1], F32)

        t_ld = tick("t_ld", 16)
        t_wld = tick("t_wld", 16); t_cq = tick("t_cq")
        t_lq = tick("t_lq", 16); t_wb = tick("t_wb")
        t_misc = tick("t_misc"); t_tp = tick("t_tp"); t_tv = tick("t_tv")
        t_tab = tick("t_tab"); t_ab = tick("t_ab", 16)
        t_gld = tick("t_gld", 16)
        t_ldx = tick("t_ldx", 16)
        t_sq0 = tick("t_sq0"); t_st0 = tick("t_st0"); t_sqrt0 = tick("t_sqrt0")
        t_is0 = tick("t_is0"); t_xn0 = tick("t_xn0")
        t_T0 = tick("t_T0"); t_eT0 = tick("t_eT0")
        t_ffn = tick("t_ffn"); t_x1 = tick("t_x1"); t_stx1 = tick("t_stx1", 16)
        t_sq1 = tick("t_sq1"); t_st1 = tick("t_st1"); t_sqrt1 = tick("t_sqrt1")
        t_is1 = tick("t_is1"); t_xn1 = tick("t_xn1")
        t_T1 = tick("t_T1"); t_eT1 = tick("t_eT1")
        t_qkk = tick("t_qkk"); t_emin = tick("t_emin"); t_exp = tick("t_exp")
        t_estt = tick("t_estt"); t_csbc = tick("t_csbc", 16); t_cs6 = tick("t_cs6")
        t_rope = tick("t_rope"); t_kv = tick("t_kv"); t_ks = tick("t_ks")
        t_aend = tick("t_aend"); t_kmbd = tick("t_kmbd", 16)
        u_ldx = tick("u_ldx", 16)
        u_sq1 = tick("u_sq1"); u_st1 = tick("u_st1"); u_sqrt1 = tick("u_sqrt1")
        u_is1 = tick("u_is1"); u_xn1 = tick("u_xn1")
        u_T1 = tick("u_T1"); u_eT1 = tick("u_eT1")
        u_qkq = tick("u_qkq"); u_emin = tick("u_emin"); u_exp = tick("u_exp")
        u_estt = tick("u_estt"); u_csbc = tick("u_csbc", 16); u_cs6 = tick("u_cs6")
        u_rope = tick("u_rope"); u_Tq = tick("u_Tq"); u_eTq = tick("u_eTq")
        u_dots = tick("u_dots"); u_attn = tick("u_attn"); u_lepe = tick("u_lepe")
        u_epi1 = tick("u_epi1"); u_epi2 = tick("u_epi2")
        u_sq2 = tick("u_sq2"); u_st2 = tick("u_st2"); u_sqrt2 = tick("u_sqrt2")
        u_is2 = tick("u_is2"); u_xn2 = tick("u_xn2")
        u_T2 = tick("u_T2"); u_eT2 = tick("u_eT2")
        u_ffn2 = tick("u_ffn2"); u_x3 = tick("u_x3"); u_out = tick("u_out", 16)
        u_pad = tick("u_pad")
        t_kev = tick("t_kev"); u_qev = tick("u_qev")
        vself = tick("vself")
        _vc = [0]

        def vg(eng, inst):
            _vc[0] += 1
            inst.then_inc(vself.sem, 1)
            eng.wait_ge(vself.sem, _vc[0])
            return inst

        sself = tick("sself")
        _sc = [0]

        def sg(eng, inst):
            _sc[0] += 1
            inst.then_inc(sself.sem, 1)
            eng.wait_ge(sself.sem, _sc[0])
            return inst

        with ExitStack() as ast:
            asb = lambda n, s, d=F32: ast.enter_context(nc.sbuf_tensor(n, s, d))
            aps = lambda n, s, d=F32: ast.enter_context(nc.psum_tensor(n, s, d))

            kmrow_sb = asb("kmrow_sb", [H, P], BF16)
            ksf = asb("ksf", [P, H], F32)
            ksbf = asb("ksbf", [P, H], BF16)
            Wqk_k = asb("Wqk_k", [P, NK, C], BF16)
            qkbk_row = asb("qkbk_row", [1, C], BF16)
            qk_stg = asb("qk_stg", [P, 2, C], BF16)
            xa = asb("xa", [P, 4, C], F32)
            xn0b = asb("xn0b", [P, 2, C], BF16)
            xn0T = asb("xn0T", [P, 2, C], BF16)
            xn1b = asb("xn1b", [P, 4, C], BF16)
            xn1T = asb("xn1T", [P, 2, C], BF16)
            mnb = asb("mnb", [P, C], BF16)
            eb = asb("eb", [P, C], BF16)
            kpb = asb("kpb", [P, 2, C], BF16)
            krb = asb("krb", [P, 2, C], BF16)
            cab = asb("cab", [P, 2, C], BF16)
            cst = asb("cst", [P, 2, C], BF16)
            vtmp = asb("vtmp", [P, C], BF16)
            sqg = asb("sqg", [P, C], BF16)
            sm = asb("sm", [P, 8], F32)
            sn = asb("sn", [P, 8], F32)
            sm0 = asb("sm0", [P, 8], F32)
            sn0 = asb("sn0", [P, 8], F32)

            p_y = aps("p_y", [P, C], F32)
            p_k = aps("p_k", [P, C], F32)
            p_kv = aps("p_kv", [P, C], F32)
            p_ks = aps("p_ks", [P, H], F32)
            p_t = aps("p_t", [P, C], BF16)

            sst = ExitStack()
            ssb = lambda n, s, d=F32: sst.enter_context(nc.sbuf_tensor(n, s, d))
            stg = ssb("stg", [P, 2, C], F32)
            VrS = ssb("VrS", [P, NK * R], BF16)
            CS = ssb("CS", [P, NK * R], BF16)
            VrT = ssb("VrT", [R, C], BF16)
            CT = ssb("CT", [R, C], BF16)
            idn = ssb("idn", [P, P], I32)
            tabf = ssb("tabf", [P, C], F32)
            tabi = ssb("tabi", [P, 512], I32)
            tabs2 = ssb("tabs2", [P, 512], F32)
            atf = ssb("atf", [nt, C], BF16)
            throw = ssb("throw", [1, 512], F32)
            posr = ssb("posr", [1, P], F32)
            tbr = ssb("tbr", [1, nt], F32)
            rows4 = ssb("rows4", [1, 2 * C], BF16)
            c_half = ssb("c_half", [P, 1], F32)

            with nc.Block() as blk:
                @blk.gpsimd
                def _(eng):
                    eng.iota(idn[:, :], [[1, P]], channel_multiplier=-1,
                             allow_small_or_imprecise_dtypes=True)
                    eng.iota(throw[:, 0:512], [[1, 512]], channel_multiplier=0,
                             allow_small_or_imprecise_dtypes=True)
                    eng.iota(posr[:, :], [[1, P]], channel_multiplier=0,
                             allow_small_or_imprecise_dtypes=True)
                    eng.iota(tbr[:, :], [[P, nt]], channel_multiplier=0,
                             allow_small_or_imprecise_dtypes=True)
                    eng.memset(ones1[:, :], 1.0)
                    eng.memset(ones_col[:, :], 1.0)
                    eng.memset(c_eps[:, :], EPS)
                    eng.memset(c_nhalf[:, :], -0.5)
                    eng.memset(c_inv64[:, :], 1.0 / float(np.sqrt(T)))
                    eng.memset(c_npi[:, :], -float(np.pi))
                    eng.memset(c_2pi[:, :], 2.0 * float(np.pi))
                    eng.memset(c_s1[:, :], -float(np.log(ROPE_BASE)) / 512.0)
                    eng.memset(c_half[:, :], 0.5)
            nc.all_engine_barrier()

            with nc.Block() as blk:
                @blk.sync
                def _(eng):
                    for pl in range(4):
                        t_ldx.inc(eng.dma_start(xa[:, pl, :],
                                                x_d[pl * P:(pl + 1) * P, :]))
                    eng.dma_start(stg[:, 0, 0:NK * R],
                                  bass.AP(Vr_d.tensor, 0,
                                          [[R, P], [P * R, NK], [1, R]])
                                  ).then_inc(t_ld.sem, 16)
                    eng.dma_start(stg[:, 1, 0:NK * R],
                                  bass.AP(Cm_d.tensor, 0,
                                          [[R, P], [P * R, NK], [1, R]])
                                  ).then_inc(t_ld.sem, 16)
                    for k in range(NK):
                        if k < 2:
                            t_misc.wait(eng, 1)
                        else:
                            t_cq.wait(eng, k - 1)
                        eng.dma_start(stg[:, k % 2, :],
                                      qkw_d[k * P:(k + 1) * P, C:2 * C]
                                      ).then_inc(t_ld.sem, 16)

                @blk.vector
                def _(eng):
                    t_ld.wait(eng, 2)
                    t_gld.wait(eng, 2)
                    eng.tensor_scalar_mul(VrS[:, :], stg[:, 0, 0:NK * R], 0.5)
                    eng.tensor_copy(CS[:, :], stg[:, 1, 0:NK * R])
                    eng.tensor_scalar_mul(bc_row[:, :], rows4[0:1, 0:C], 0.5)
                    eng.tensor_copy(lepb_row[:, :], rows4[0:1, C:2 * C])
                    vg(eng, eng.tensor_single_scalar(ident[:, :], idn[:, :],
                                                     0.0, ALU.is_equal))
                    t_misc.inc(eng.tensor_copy(tabs2[0:1, 0:1],
                                               ident[0:1, 0:1]))
                    t_gld.wait(eng, 3)
                    eng.tensor_copy(
                        qkbq_row[:, :],
                        rows4[0:1, 0:C].rearrange("p (n two) -> p n two",
                                                  two=2).transpose([0, 2, 1]))
                    t_misc.inc(eng.tensor_copy(
                        qkbk_row[:, :],
                        rows4[0:1, C:2 * C].rearrange("p (n two) -> p n two",
                                                      two=2).transpose([0, 2, 1])))

                @blk.gpsimd
                def _(eng):
                    eng.dma_start(rows4[0:1, 0:C], b_d[None, :]).then_inc(t_gld.sem, 16)
                    eng.dma_start(rows4[0:1, C:2 * C],
                                  lepb_d[None, :]).then_inc(t_gld.sem, 16)
                    t_misc.wait(eng, 1)
                    eng.dma_start(rows4[0:1, 0:2 * C],
                                  qkb_d[None, :]).then_inc(t_gld.sem, 16)
                    for k in range(NK):
                        if k >= 2:
                            t_tp.wait(eng, k + 1)
                        eng.dma_start(qk_stg[:, k % 2, :],
                                      W_d[k * P:(k + 1) * P, :]
                                      ).then_inc(t_wld.sem, 16)
                    for k in range(NK):
                        t_ld.wait(eng, 2 + k + 1)
                        t_cq.inc(eng.tensor_copy(
                            Wqk_k[:, k, :],
                            stg[:, k % 2, :].rearrange(
                                "p (n two) -> p n two", two=2
                                ).transpose([0, 2, 1])))

                @blk.tensor
                def _(eng):
                    t_misc.wait(eng, 1)
                    for a in range(NK):
                        inst = eng.transpose(p_t[0:R, a * P:(a + 1) * P],
                                             VrS[:, a * R:(a + 1) * R],
                                             ident[:, :])
                    t_tp.inc(inst)

                @blk.vector
                def _(eng):
                    t_tp.wait(eng, 1)
                    t_tv.inc(eng.tensor_copy(VrT[:, :], p_t[0:R, :]))

                @blk.tensor
                def _(eng):
                    t_tv.wait(eng, 1)
                    for a in range(NK):
                        inst = eng.transpose(p_t[0:R, a * P:(a + 1) * P],
                                             CS[:, a * R:(a + 1) * R],
                                             ident[:, :])
                    t_tp.inc(inst)

                @blk.vector
                def _(eng):
                    t_tp.wait(eng, 2)
                    t_tv.inc(eng.tensor_copy(CT[:, :], p_t[0:R, :]))

                for r2 in range(NK):
                    @blk.tensor
                    def _(eng, r2=r2):
                        t_wld.wait(eng, r2 + 1)
                        if r2 == 0:
                            t_tv.wait(eng, 2)
                        else:
                            t_tv.wait(eng, 2 + r2)
                        for cch in range(NK):
                            inst = eng.transpose(p_t[:, cch * P:(cch + 1) * P],
                                                 qk_stg[:, r2 % 2,
                                                        cch * P:(cch + 1) * P],
                                                 ident[:, :])
                        t_tp.inc(inst)

                    @blk.vector
                    def _(eng, r2=r2):
                        t_tp.wait(eng, 3 + r2)
                        t_tv.inc(eng.tensor_scalar_mul(
                            Wc[:, :, r2 * P:(r2 + 1) * P],
                            p_t[:, :].rearrange("p (k f) -> p k f", k=NK),
                            0.5))

                if True:
                    p_w = p_y
                    for cch in range(NK):
                        @blk.tensor
                        def _(eng, cch=cch):
                            t_tv.wait(eng, 2 + NK + cch)
                            eng.matmul(p_w[:, 0:512],
                                       VrT[:, cch * P:(cch + 1) * P],
                                       CT[:, 0:512], start=True, stop=True)
                            t_tp.inc(eng.matmul(p_w[:, 512:C],
                                                VrT[:, cch * P:(cch + 1) * P],
                                                CT[:, 512:C],
                                                start=True, stop=True))

                        @blk.vector
                        def _(eng, cch=cch):
                            t_tp.wait(eng, 3 + NK + cch)
                            t_tv.inc(eng.tensor_tensor(Wc[:, cch, :],
                                                       Wc[:, cch, :],
                                                       p_w[:, :], ALU.add))

                    @blk.scalar
                    def _(eng):
                        t_tab.inc(eng.activation(throw[0:1, 0:512],
                                                 throw[0:1, 0:512], ACT.Exp,
                                                 scale=c_s1[0:1, :]))

                    @blk.vector
                    def _(eng):
                        t_tab.wait(eng, 1)
                        t_tab.inc(eng.tensor_scalar_mul(
                            throw[0:1, 0:512], throw[0:1, 0:512],
                            1.0 / (2.0 * float(np.pi))))

                    @blk.tensor
                    def _(eng):
                        t_tab.wait(eng, 2)
                        t_tv.wait(eng, 2 + 2 * NK)
                        eng.matmul(p_w[0:nt, 0:512], tbr[:, :],
                                   throw[0:1, 0:512], start=True, stop=True)
                        t_tab.inc(eng.matmul(p_w[0:nt, 512:C], tbr[:, :],
                                             throw[0:1, 0:512],
                                             start=True, stop=True))

                    def rangered(eng, pang, rows):
                        vg(eng, eng.tensor_scalar_add(tabf[0:rows, 0:512],
                                                      pang[0:rows, 0:512], 0.25))
                        vg(eng, eng.tensor_scalar_add(tabf[0:rows, 512:C],
                                                      pang[0:rows, 512:C], 0.0))
                        for hh in range(2):
                            slh = slice(512 * hh, 512 * (hh + 1))
                            vg(eng, eng.tensor_copy(tabi[0:rows, :],
                                                    tabf[0:rows, slh]))
                            vg(eng, eng.tensor_copy(tabs2[0:rows, :],
                                                    tabi[0:rows, :]))
                            vg(eng, eng.tensor_sub(tabf[0:rows, slh],
                                                   tabf[0:rows, slh],
                                                   tabs2[0:rows, :]))
                        vg(eng, eng.tensor_single_scalar(tabs2[0:rows, :],
                                                         tabf[0:rows, 0:512],
                                                         0.5, ALU.is_ge))
                        vg(eng, eng.tensor_sub(tabf[0:rows, 0:512],
                                               tabf[0:rows, 0:512],
                                               tabs2[0:rows, :]))
                        vg(eng, eng.tensor_single_scalar(tabs2[0:rows, :],
                                                         tabf[0:rows, 512:C],
                                                         0.5, ALU.is_ge))
                        vg(eng, eng.tensor_sub(tabf[0:rows, 512:C],
                                               tabf[0:rows, 512:C],
                                               tabs2[0:rows, :]))
                        vg(eng, eng.tensor_single_scalar(tabf[0:rows, :],
                                                         tabf[0:rows, :],
                                                         -0.4999995, ALU.max))
                        vg(eng, eng.tensor_single_scalar(tabf[0:rows, :],
                                                         tabf[0:rows, :],
                                                         0.4999995, ALU.min))

                    @blk.vector
                    def _(eng):
                        t_tab.wait(eng, 3)
                        rangered(eng, p_w, nt)
                        t_tab.inc(eng.engine_nop())

                    @blk.scalar
                    def _(eng):
                        t_tab.wait(eng, 4)
                        t_tab.inc(eng.activation(atf[:, :], tabf[0:nt, :],
                                                 ACT.Sin,
                                                 scale=c_2pi[0:nt, :]))

                    @blk.sync
                    def _(eng):
                        t_tab.wait(eng, 5)
                        eng.dma_start(atab_d, atf[:, :]).then_inc(t_ab.sem, 16)

                    @blk.tensor
                    def _(eng):
                        t_tab.wait(eng, 5)
                        eng.matmul(p_w[:, 0:512], posr[:, :], throw[0:1, 0:512],
                                   start=True, stop=True)
                        t_tab.inc(eng.matmul(p_w[:, 512:C], posr[:, :],
                                             throw[0:1, 0:512],
                                             start=True, stop=True))

                    @blk.vector
                    def _(eng):
                        t_tab.wait(eng, 6)
                        rangered(eng, p_w, P)
                        t_tab.inc(eng.engine_nop())

                    @blk.scalar
                    def _(eng):
                        t_tab.wait(eng, 7)
                        t_tab.inc(eng.activation(Btab[:, :], tabf[:, :],
                                                 ACT.Sin,
                                                 scale=c_2pi[:, :]))

                @blk.vector
                def _(eng):
                    eng.memset(p_kv[:, :], 0.0)
                    eng.memset(p_ks[:, :], 0.0)

            sst.close()

            def A_iter(blk, i):
                jl = i + 3
                jc = i - 2
                je = i - 1
                jq = i - 2
                jk = i - 3
                jx = i - 4
                jf = i + 1

                @blk.sync
                def _(eng):
                    if 4 <= jl < nt:
                        t_stx1.wait(eng, jl - 3)
                        t_xn1.wait(eng, jl - 3)
                        t_ldx.inc(eng.dma_start(xa[:, jl % 4, :],
                                                x_d[jl * P:(jl + 1) * P, :]))
                    if 0 <= jc < nt:
                        t_ab.wait(eng, 1)
                        if jc >= 2:
                            t_cs6.wait(eng, jc - 1)
                        t_csbc.inc(eng.dma_start(
                            cab[:, jc % 2, :],
                            bass.AP(atab_d.tensor, jc * C, [[0, P], [1, C]])))
                    if 0 <= i < nt:
                        t_x1.wait(eng, i + 1)
                        t_stx1.inc(eng.dma_start(x1_d[i * P:(i + 1) * P, :],
                                                 xa[:, i % 4, :]))

                @blk.vector
                def _(eng):
                    if 0 <= jq < nt:
                        t_T1.wait(eng, jq + 1)
                        t_qkk.wait(eng, jq - 1)
                        t_eT1.inc(eng.tensor_copy(xn1T[:, jq % 2, :],
                                                  p_t[:, :]))
                    if 0 <= je < nt:
                        q = je % 2
                        t_x1.wait(eng, je + 1)
                        t_st1.wait(eng, je - 1)
                        t_sq1.inc(eng.scalar_tensor_tensor(
                            vtmp[:, :], xa[:, je % 4, :], 1.0,
                            xa[:, je % 4, :], ALU.mult, ALU.mult,
                            accum_out=sn[:, q:q + 1]))
                    if 0 <= je < nt:
                        q = je % 2
                        t_sq1.wait(eng, je + 1)
                        vg(eng, eng.tensor_mul(sm[:, 4 + q:5 + q],
                                               sm[:, 2 + q:3 + q],
                                               sm[:, 2 + q:3 + q]))
                        t_st1.inc(eng.scalar_tensor_tensor(
                            sm[:, 4 + q:5 + q], sn[:, q:q + 1], 1.0 / C,
                            sm[:, 4 + q:5 + q], ALU.mult, ALU.subtract))
                    if 0 <= je < nt:
                        q = je % 2
                        t_sqrt1.wait(eng, je + 1)
                        t_xn1.wait(eng, je - 1)
                        t_is1.inc(eng.tensor_scalar(
                            sn[:, 2 + q:3 + q], sm[:, 2 + q:3 + q],
                            sm[:, 6 + q:7 + q], -1.0, ALU.mult, ALU.mult))
                    if 0 <= i < nt:
                        t_ffn.wait(eng, i + 1)
                        t_xn0.wait(eng, i + 1)
                        t_x1.inc(eng.tensor_tensor(xa[:, i % 4, :],
                                                   xa[:, i % 4, :], p_y[:, :],
                                                   ALU.add))
                    if 0 <= i < nt:
                        q = i % 2
                        t_st1.wait(eng, i - 1)
                        vg(eng, eng.tensor_reduce(sm[:, q:q + 1],
                                                  xa[:, i % 4, :],
                                                  AX.X, ALU.add))
                        vg(eng, eng.tensor_scalar_mul(sm[:, 2 + q:3 + q],
                                                      sm[:, q:q + 1], 1.0 / C))
                    if 0 <= jf < nt:
                        t_T0.wait(eng, jf + 1)
                        t_ffn.wait(eng, jf - 1)
                        t_eT0.inc(eng.tensor_copy(xn0T[:, jf % 2, :],
                                                  p_t[:, :]))
                    if 0 <= jx < nt:
                        t_kev.wait(eng, jx + 1)
                        t_exp.wait(eng, jx - 1)
                        t_emin.inc(eng.tensor_single_scalar(
                            mnb[:, :], kpb[:, jx % 2, :], 0.0, ALU.min))
                    if 0 <= jx < nt:
                        t_exp.wait(eng, jx + 1)
                        t_estt.inc(eng.scalar_tensor_tensor(
                            kpb[:, jx % 2, :], kpb[:, jx % 2, :], 0.0,
                            eb[:, :], ALU.max, ALU.add))
                    if 0 <= jx < nt:
                        t_cs6.wait(eng, jx + 1)
                        t_kv.wait(eng, jx - 1)
                        cs = cst[:, jx % 2, :]
                        kp = kpb[:, jx % 2, :]
                        eng.tensor_mul(vtmp[:, 0:512], cs[:, 0:512], kp[:, 0:512])
                        eng.tensor_mul(vtmp[:, 512:C], cs[:, 512:C], kp[:, 512:C])
                        eng.tensor_sub(krb[:, jx % 2, 0:512], vtmp[:, 0:512],
                                       vtmp[:, 512:C])
                        eng.tensor_mul(vtmp[:, 0:512], cs[:, 0:512], kp[:, 512:C])
                        eng.tensor_mul(vtmp[:, 512:C], cs[:, 512:C], kp[:, 0:512])
                        t_rope.inc(eng.tensor_add(krb[:, jx % 2, 512:C],
                                                  vtmp[:, 0:512],
                                                  vtmp[:, 512:C]))
                    if 0 <= jk < nt:
                        if jk == 0:
                            t_tab.wait(eng, 8)
                        t_csbc.wait(eng, jk + 1)
                        t_rope.wait(eng, jk - 1)
                        ca = cab[:, jk % 2, :]
                        cd = cst[:, jk % 2, :]
                        eng.tensor_mul(vtmp[:, 0:512], ca[:, 0:512], Btab[:, 0:512])
                        eng.tensor_mul(vtmp[:, 512:C], ca[:, 512:C], Btab[:, 512:C])
                        eng.tensor_sub(cd[:, 0:512], vtmp[:, 0:512],
                                       vtmp[:, 512:C])
                        eng.tensor_mul(vtmp[:, 0:512], ca[:, 512:C], Btab[:, 0:512])
                        eng.tensor_mul(vtmp[:, 512:C], ca[:, 0:512], Btab[:, 512:C])
                        t_cs6.inc(eng.tensor_add(cd[:, 512:C], vtmp[:, 0:512],
                                                 vtmp[:, 512:C]))
                    if 0 <= jl < nt:
                        q = jl % 2
                        t_ldx.wait(eng, jl + 1)
                        t_xn0.wait(eng, jl - 1)
                        vg(eng, eng.tensor_reduce(sm0[:, q:q + 1],
                                                  xa[:, jl % 4, :],
                                                  AX.X, ALU.add))
                        vg(eng, eng.tensor_scalar_mul(sm0[:, 2 + q:3 + q],
                                                      sm0[:, q:q + 1], 1.0 / C))
                        t_sq0.wait(eng, jl + 1)
                        vg(eng, eng.tensor_mul(sm0[:, 4 + q:5 + q],
                                               sm0[:, 2 + q:3 + q],
                                               sm0[:, 2 + q:3 + q]))
                        t_st0.inc(eng.scalar_tensor_tensor(
                            sm0[:, 4 + q:5 + q], sn0[:, q:q + 1], 1.0 / C,
                            sm0[:, 4 + q:5 + q], ALU.mult, ALU.subtract))
                    if 0 <= jl < nt:
                        q = jl % 2
                        t_sqrt0.wait(eng, jl + 1)
                        t_is0.inc(eng.tensor_scalar(
                            sn0[:, 2 + q:3 + q], sm0[:, 2 + q:3 + q],
                            sm0[:, 6 + q:7 + q], -1.0, ALU.mult, ALU.mult))

                @blk.scalar
                def _(eng):
                    if 0 <= je < nt:
                        q = je % 2
                        t_st1.wait(eng, je + 1)
                        t_is1.wait(eng, je - 1)
                        sg(eng, eng.activation(sn[:, 6 + q:7 + q],
                                               sm[:, 4 + q:5 + q],
                                               ACT.Ln, bias=c_eps[:, :]))
                        t_sqrt1.inc(eng.activation(
                            sm[:, 6 + q:7 + q], sn[:, 6 + q:7 + q],
                            ACT.Exp, scale=c_nhalf[:, :]))
                        t_sqrt1.wait(eng, je + 1)
                    if 0 <= je < nt:
                        q = je % 2
                        t_is1.wait(eng, je + 1)
                        t_kv.wait(eng, je - 3)
                        t_xn1.inc(eng.activation(xn1b[:, je % 4, :],
                                                 xa[:, je % 4, :],
                                                 ACT.Identity,
                                                 bias=sn[:, 2 + q:3 + q],
                                                 scale=sm[:, 6 + q:7 + q]))
                    if 0 <= jk < nt:
                        t_qkk.wait(eng, jk + 1)
                        t_estt.wait(eng, jk - 1)
                        t_ks.wait(eng, jk - 1)
                        t_kev.inc(eng.activation(kpb[:, jk % 2, :], p_k[:, :],
                                                 ACT.Copy))
                    if 0 <= jx < nt:
                        t_emin.wait(eng, jx + 1)
                        t_estt.wait(eng, jx - 1)
                        t_exp.inc(eng.activation(eb[:, :], mnb[:, :], ACT.Exp))
                    if 0 <= jl < nt:
                        q = jl % 2
                        t_ldx.wait(eng, jl + 1)
                        t_st0.wait(eng, jl - 1)
                        t_sq0.inc(eng.activation(sqg[:, :], xa[:, jl % 4, :],
                                                 ACT.Square,
                                                 accum_out=sn0[:, q:q + 1]))
                    if 0 <= jl < nt:
                        q = jl % 2
                        t_st0.wait(eng, jl + 1)
                        t_is0.wait(eng, jl - 1)
                        sg(eng, eng.activation(sn0[:, 6 + q:7 + q],
                                               sm0[:, 4 + q:5 + q],
                                               ACT.Ln, bias=c_eps[:, :]))
                        t_sqrt0.inc(eng.activation(
                            sm0[:, 6 + q:7 + q], sn0[:, 6 + q:7 + q],
                            ACT.Exp, scale=c_nhalf[:, :]))
                        t_sqrt0.wait(eng, jl + 1)
                    if 0 <= jl < nt:
                        q = jl % 2
                        t_is0.wait(eng, jl + 1)
                        t_T0.wait(eng, jl - 1)
                        t_xn0.inc(eng.activation(xn0b[:, q, :], xa[:, jl % 4, :],
                                                 ACT.Identity,
                                                 bias=sn0[:, 2 + q:3 + q],
                                                 scale=sm0[:, 6 + q:7 + q]))

                @blk.tensor
                def _(eng):
                    if 0 <= i < nt:
                        if i == 0:
                            t_tab.wait(eng, 7)
                        t_eT0.wait(eng, i + 1)
                        t_x1.wait(eng, i)
                        for h2 in range(2):
                            sl = slice(512 * h2, 512 * (h2 + 1))
                            for k in range(NK):
                                eng.matmul(p_y[:, sl],
                                           xn0T[:, i % 2, k * P:(k + 1) * P],
                                           Wc[:, k, sl],
                                           start=(k == 0), stop=False)
                            inst = eng.matmul(p_y[:, sl], ones1[:, :],
                                              bc_row[:, sl],
                                              start=False, stop=True)
                        t_ffn.inc(inst)
                    if 0 <= jf < nt:
                        if jf == 0:
                            t_tv.wait(eng, 2 + NK)
                        t_xn0.wait(eng, jf + 1)
                        t_eT1.wait(eng, jf - 2)
                        for k in range(NK):
                            inst = eng.transpose(p_t[:, k * P:(k + 1) * P],
                                                 xn0b[:, jf % 2,
                                                      k * P:(k + 1) * P],
                                                 ident[:, :])
                        t_T0.inc(inst)
                    if 0 <= jq < nt:
                        if jq == 0:
                            t_cq.wait(eng, NK)
                        t_eT1.wait(eng, jq + 1)
                        t_kev.wait(eng, jq)
                        for h2 in range(2):
                            sl = slice(512 * h2, 512 * (h2 + 1))
                            for k in range(NK):
                                eng.matmul(p_k[:, sl],
                                           xn1T[:, jq % 2, k * P:(k + 1) * P],
                                           Wqk_k[:, k, sl],
                                           start=(k == 0), stop=False)
                            inst = eng.matmul(p_k[:, sl], ones1[:, :],
                                              qkbk_row[:, sl],
                                              start=False, stop=True)
                        t_qkk.inc(inst)
                    if 0 <= je < nt:
                        t_xn1.wait(eng, je + 1)
                        t_eT0.wait(eng, min(je + 3, nt))
                        for k in range(NK):
                            inst = eng.transpose(p_t[:, k * P:(k + 1) * P],
                                                 xn1b[:, je % 4,
                                                      k * P:(k + 1) * P],
                                                 ident[:, :])
                        t_T1.inc(inst)
                    if 0 <= jx < nt:
                        t_rope.wait(eng, jx + 1)
                        t_xn1.wait(eng, jx + 1)
                        for h in range(H):
                            inst = eng.matmul(p_kv[:, h * P:(h + 1) * P],
                                              krb[:, jx % 2, h * P:(h + 1) * P],
                                              xn1b[:, jx % 4, h * P:(h + 1) * P],
                                              start=False, stop=True,
                                              skip_group_check=True)
                        t_kv.inc(inst)
                    if 0 <= jx < nt:
                        t_estt.wait(eng, jx + 1)
                        for k in range(NK):
                            inst = eng.matmul(p_ks[:, k:k + 1],
                                              kpb[:, jx % 2, k * P:(k + 1) * P],
                                              ones_col[:, :],
                                              start=False, stop=True,
                                              skip_group_check=True)
                        t_ks.inc(inst)

            def emit_deferred_loads(blk):
                @blk.gpsimd
                def _(eng):
                    t_tp.wait(eng, 2 + NK)
                    eng.dma_start(qk_stg[:, 0, :],
                                  qkw_d[0:P, 0:C]).then_inc(t_lq.sem, 16)
                    eng.dma_start(qk_stg[:, 1, :],
                                  qkw_d[P:2 * P, 0:C]).then_inc(t_lq.sem, 16)
                    for ii in range(NK):
                        t_lq.wait(eng, ii + 1)
                        eng.tensor_copy(
                            Wqk_q[:, ii, :],
                            qk_stg[:, ii % 2, :].rearrange(
                                "p (n two) -> p n two", two=2
                                ).transpose([0, 2, 1])).then_inc(t_wb.sem, 1)
                        if ii + 2 < NK:
                            t_wb.wait(eng, ii + 1)
                            eng.dma_start(qk_stg[:, ii % 2, :],
                                          qkw_d[(ii + 2) * P:(ii + 3) * P, 0:C]
                                          ).then_inc(t_lq.sem, 16)
                    for t3 in range(3):
                        for k in range(NK):
                            eng.dma_start(Wlep[:, t3, k, :],
                                          lepw_d[t3, k * P:(k + 1) * P, :]
                                          ).then_inc(t_lq.sem, 16)

            if serial:
                with nc.Block() as blk:
                    emit_deferred_loads(blk)
                nc.all_engine_barrier()
                for it in range(-3, nt + 4):
                    with nc.Block() as blk:
                        A_iter(blk, it)
                    nc.all_engine_barrier()
                blk = nc.Block().__enter__()
            else:
                blk = nc.Block().__enter__()
                emit_deferred_loads(blk)
                for it in range(-3, nt + 4):
                    A_iter(blk, it)
            if True:

                @blk.scalar
                def _(eng):
                    t_kv.wait(eng, nt)
                    eng.activation(kv_sb[:, 0:512], p_kv[:, 0:512], ACT.Copy,
                                   scale=c_inv64[:, :])
                    t_aend.inc(eng.activation(kv_sb[:, 512:C], p_kv[:, 512:C],
                                              ACT.Copy, scale=c_inv64[:, :]))

                @blk.vector
                def _(eng):
                    t_ks.wait(eng, nt)
                    vg(eng, eng.tensor_scalar_mul(ksf[:, :], p_ks[:, :], 1.0 / T))
                    t_aend.inc(eng.tensor_copy(ksbf[:, :], ksf[:, :]))

                @blk.tensor
                def _(eng):
                    t_aend.wait(eng, 2)
                    t_eT1.wait(eng, nt)
                    t_aend.inc(eng.transpose(p_t[0:H, 0:P], ksbf[:, :],
                                             ident[:, :]))

                @blk.scalar
                def _(eng):
                    t_aend.wait(eng, 3)
                    t_aend.inc(eng.activation(kmrow_sb[:, :], p_t[0:H, 0:P],
                                              ACT.Copy))

                @blk.sync
                def _(eng):
                    t_aend.wait(eng, 4)
                    eng.dma_start(kmrow_d.rearrange("one (p f) -> (one p) f",
                                                     p=H),
                                  kmrow_sb[:, :]).then_inc(t_kmbd.sem, 16)
                    t_kmbd.wait(eng, 1)
                    eng.dma_start(kmb[:, :],
                                  bass.AP(kmrow_d.tensor, 0, [[0, P], [1, C]])
                                  ).then_inc(t_kmbd.sem, 16)
                    t_kmbd.wait(eng, 2)
            blk.__exit__(None, None, None)

            nc.all_engine_barrier()

        with ExitStack() as bst:
            bsb = lambda n, s, d=F32: bst.enter_context(nc.sbuf_tensor(n, s, d))
            bps = lambda n, s, d=F32: bst.enter_context(nc.psum_tensor(n, s, d))
            xb = bsb("xb", [P, 4, C], F32)
            xn1c = bsb("xn1c", [P, 2, C], BF16)
            xnT_all = bsb("xnT_all", [P, NK, nt * P + 2], BF16)
            qpb = bsb("qpb", [P, 2, C], BF16)
            mnb2 = bsb("mnb2", [P, C], BF16)
            eb2 = bsb("eb2", [P, C], BF16)
            cab2 = bsb("cab2", [P, 2, C], BF16)
            cst2 = bsb("cst2", [P, 1, C], BF16)
            vtmp2 = bsb("vtmp2", [P, C], BF16)
            qrb = bsb("qrb", [P, C], BF16)
            qrT = bsb("qrT", [P, 2, C], BF16)
            xn2b = bsb("xn2b", [P, 2, C], BF16)
            sqg2 = bsb("sqg2", [P, C], BF16)
            sm2 = bsb("sm2", [P, 8], F32)
            sn2 = bsb("sn2", [P, 8], F32)
            sm3 = bsb("sm3", [P, 8], F32)
            sn3 = bsb("sn3", [P, 8], F32)
            dsg = bsb("dsg", [P, 16], F32)
            zb = bsb("zb", [P, 2 * H], F32)
            xn2T = bsb("xn2T", [P, C], BF16)

            p_qa = bps("p_qa", [P, C], F32)
            p_lep = bps("p_lep", [P, C], F32)
            p_y2 = bps("p_y2", [P, C], F32)
            p_t2 = bps("p_t2", [P, C], BF16)
            p_tq = bps("p_tq", [P, C], BF16)

            def B_iter(blk, i):
                jp, j, ja, jb, jc = i + 1, i, i - 1, i - 2, i - 3

                @blk.sync
                def _(eng):
                    if 0 <= jc < nt:
                        u_x3.wait(eng, jc + 1)
                        u_out.inc(eng.dma_start(out_d[jc * P:(jc + 1) * P, :],
                                                xb[:, jc % 4, :]))
                    if 0 <= jp < nt:
                        if jp >= 4:
                            u_out.wait(eng, jp - 3)
                        u_ldx.inc(eng.dma_start(xb[:, jp % 4, :],
                                                x1_d[jp * P:(jp + 1) * P, :]))
                    if 0 <= j < nt:
                        if j >= 2:
                            u_cs6.wait(eng, j - 1)
                        u_csbc.inc(eng.dma_start(
                            cab2[:, j % 2, :],
                            bass.AP(atab_d.tensor, j * C, [[0, P], [1, C]])))

                @blk.vector
                def _(eng):
                    if i == -1:
                        eng.memset(xnT_all[:, :, 0:1], 0.0)
                        u_pad.inc(eng.memset(
                            xnT_all[:, :, nt * P + 1:nt * P + 2], 0.0))
                    if 0 <= ja < nt:
                        u_qev.wait(eng, ja + 1)
                        u_exp.wait(eng, ja)
                        u_emin.inc(eng.tensor_single_scalar(
                            mnb2[:, :], qpb[:, ja % 2, :], 0.0, ALU.min))
                        u_exp.wait(eng, ja + 1)
                        u_estt.inc(eng.scalar_tensor_tensor(
                            qpb[:, ja % 2, :], qpb[:, ja % 2, :], 0.0,
                            eb2[:, :], ALU.max, ALU.add))
                    if 0 <= ja < nt:
                        u_csbc.wait(eng, ja + 1)
                        ca = cab2[:, ja % 2, :]
                        cd = cst2[:, 0, :]
                        eng.tensor_mul(vtmp2[:, 0:512], ca[:, 0:512],
                                       Btab[:, 0:512])
                        eng.tensor_mul(vtmp2[:, 512:C], ca[:, 512:C],
                                       Btab[:, 512:C])
                        eng.tensor_sub(cd[:, 0:512], vtmp2[:, 0:512],
                                       vtmp2[:, 512:C])
                        eng.tensor_mul(vtmp2[:, 0:512], ca[:, 512:C],
                                       Btab[:, 0:512])
                        eng.tensor_mul(vtmp2[:, 512:C], ca[:, 0:512],
                                       Btab[:, 512:C])
                        u_cs6.inc(eng.tensor_add(cd[:, 512:C], vtmp2[:, 0:512],
                                                 vtmp2[:, 512:C]))
                    if 0 <= ja < nt:
                        q8 = 8 * (ja % 2)
                        eng.tensor_mul(vtmp2[:, :], qpb[:, ja % 2, :], kmb[:, :])
                        vg(eng, eng.tensor_reduce(dsg[:, :],
                                                  vtmp2[:, :].rearrange(
                                                      "p (s w) -> p s w", s=16),
                                                  AX.X, ALU.add))
                        vg(eng, eng.tensor_add(zb[:, q8:q8 + 8], dsg[:, 0:8],
                                               dsg[:, 8:16]))
                        vg(eng, eng.tensor_scalar_add(zb[:, q8:q8 + 8],
                                                      zb[:, q8:q8 + 8], 1e-6))
                        vg(eng, eng.reciprocal(zb[:, q8:q8 + 8],
                                               zb[:, q8:q8 + 8]))
                    if 0 <= ja < nt:
                        u_Tq.wait(eng, ja)
                        q8 = 8 * (ja % 2)
                        cs = cst2[:, 0, :]
                        kp = qpb[:, ja % 2, :]
                        eng.tensor_mul(vtmp2[:, 0:512], cs[:, 0:512], kp[:, 0:512])
                        eng.tensor_mul(vtmp2[:, 512:C], cs[:, 512:C], kp[:, 512:C])
                        eng.tensor_sub(qrb[:, 0:512], vtmp2[:, 0:512],
                                       vtmp2[:, 512:C])
                        eng.tensor_mul(vtmp2[:, 0:512], cs[:, 0:512], kp[:, 512:C])
                        eng.tensor_mul(vtmp2[:, 512:C], cs[:, 512:C], kp[:, 0:512])
                        vg(eng, eng.tensor_add(qrb[:, 512:C], vtmp2[:, 0:512],
                                               vtmp2[:, 512:C]))
                        for h in range(H):
                            inst = eng.tensor_scalar_mul(
                                qrb[:, h * P:(h + 1) * P],
                                qrb[:, h * P:(h + 1) * P],
                                zb[:, q8 + h:q8 + h + 1])
                        u_rope.inc(inst)
                    if 0 <= jb < nt:
                        u_attn.wait(eng, jb + 1)
                        u_epi1.inc(eng.tensor_tensor(xb[:, jb % 4, :],
                                                     xb[:, jb % 4, :],
                                                     p_lep[:, :], ALU.add))
                    if 0 <= jc < nt:
                        u_ffn2.wait(eng, jc + 1)
                        u_x3.inc(eng.tensor_tensor(xb[:, jc % 4, :],
                                                   xb[:, jc % 4, :],
                                                   p_y2[:, :], ALU.add))
                    if 0 <= jp < nt:
                        q = jp % 2
                        u_ldx.wait(eng, jp + 1)
                        vg(eng, eng.tensor_reduce(sm2[:, q:q + 1],
                                                  xb[:, jp % 4, :],
                                                  AX.X, ALU.add))
                        vg(eng, eng.tensor_scalar_mul(sm2[:, 2 + q:3 + q],
                                                      sm2[:, q:q + 1], 1.0 / C))
                        u_sq1.wait(eng, jp + 1)
                        vg(eng, eng.tensor_mul(sm2[:, 4 + q:5 + q],
                                               sm2[:, 2 + q:3 + q],
                                               sm2[:, 2 + q:3 + q]))
                        u_st1.inc(eng.scalar_tensor_tensor(
                            sm2[:, 4 + q:5 + q], sn2[:, q:q + 1], 1.0 / C,
                            sm2[:, 4 + q:5 + q], ALU.mult, ALU.subtract))
                        u_sqrt1.wait(eng, jp + 1)
                        u_is1.inc(eng.tensor_scalar(
                            sn2[:, 2 + q:3 + q], sm2[:, 2 + q:3 + q],
                            sm2[:, 6 + q:7 + q], -1.0, ALU.mult, ALU.mult))
                    if 0 <= jb < nt:
                        q = jb % 2
                        vg(eng, eng.tensor_reduce(sm3[:, q:q + 1],
                                                  xb[:, jb % 4, :],
                                                  AX.X, ALU.add))
                        vg(eng, eng.tensor_scalar_mul(sm3[:, 2 + q:3 + q],
                                                      sm3[:, q:q + 1], 1.0 / C))
                        u_sq2.wait(eng, jb + 1)
                        vg(eng, eng.tensor_mul(sm3[:, 4 + q:5 + q],
                                               sm3[:, 2 + q:3 + q],
                                               sm3[:, 2 + q:3 + q]))
                        u_st2.inc(eng.scalar_tensor_tensor(
                            sm3[:, 4 + q:5 + q], sn3[:, q:q + 1], 1.0 / C,
                            sm3[:, 4 + q:5 + q], ALU.mult, ALU.subtract))
                        u_sqrt2.wait(eng, jb + 1)
                        u_is2.inc(eng.tensor_scalar(
                            sn3[:, 2 + q:3 + q], sm3[:, 2 + q:3 + q],
                            sm3[:, 6 + q:7 + q], -1.0, ALU.mult, ALU.mult))

                @blk.scalar
                def _(eng):
                    if 0 <= j < nt:
                        u_T1.wait(eng, j + 1)
                        u_eT1.inc(eng.activation(
                            xnT_all[:, :, 1 + j * P:1 + (j + 1) * P],
                            p_t2[:, :].rearrange("p (k f) -> p k f", k=NK),
                            ACT.Copy))
                    if 0 <= ja < nt:
                        u_qkq.wait(eng, ja + 1)
                        u_rope.wait(eng, ja - 1)
                        u_qev.inc(eng.activation(qpb[:, ja % 2, :], p_qa[:, :],
                                                 ACT.Copy))
                    if 0 <= ja < nt:
                        u_emin.wait(eng, ja + 1)
                        u_estt.wait(eng, ja)
                        u_exp.inc(eng.activation(eb2[:, :], mnb2[:, :], ACT.Exp))
                    if 0 <= jc < nt:
                        u_T2.wait(eng, jc + 1)
                        u_ffn2.wait(eng, jc)
                        u_eT2.inc(eng.activation(xn2T[:, :], p_t2[:, :],
                                                 ACT.Copy))
                    if 0 <= jp < nt:
                        q = jp % 2
                        u_ldx.wait(eng, jp + 1)
                        u_st1.wait(eng, jp - 1)
                        u_sq1.inc(eng.activation(sqg2[:, :], xb[:, jp % 4, :],
                                                 ACT.Square,
                                                 accum_out=sn2[:, q:q + 1]))
                        u_st1.wait(eng, jp + 1)
                        u_is1.wait(eng, jp - 1)
                        sg(eng, eng.activation(sn2[:, 6 + q:7 + q],
                                               sm2[:, 4 + q:5 + q], ACT.Ln,
                                               bias=c_eps[:, :]))
                        u_sqrt1.inc(eng.activation(
                            sm2[:, 6 + q:7 + q], sn2[:, 6 + q:7 + q],
                            ACT.Exp, scale=c_nhalf[:, :]))
                        u_sqrt1.wait(eng, jp + 1)
                        u_is1.wait(eng, jp + 1)
                        u_T1.wait(eng, jp - 1)
                        u_xn1.inc(eng.activation(xn1c[:, q, :], xb[:, jp % 4, :],
                                                 ACT.Identity,
                                                 bias=sn2[:, 2 + q:3 + q],
                                                 scale=sm2[:, 6 + q:7 + q]))
                    if 0 <= jb < nt:
                        q = jb % 2
                        u_epi1.wait(eng, jb + 1)
                        u_st2.wait(eng, jb - 1)
                        u_sq2.inc(eng.activation(sqg2[:, :], xb[:, jb % 4, :],
                                                 ACT.Square,
                                                 accum_out=sn3[:, q:q + 1]))
                        u_st2.wait(eng, jb + 1)
                        u_is2.wait(eng, jb - 1)
                        sg(eng, eng.activation(sn3[:, 6 + q:7 + q],
                                               sm3[:, 4 + q:5 + q], ACT.Ln,
                                               bias=c_eps[:, :]))
                        u_sqrt2.inc(eng.activation(
                            sm3[:, 6 + q:7 + q], sn3[:, 6 + q:7 + q],
                            ACT.Exp, scale=c_nhalf[:, :]))
                        u_sqrt2.wait(eng, jb + 1)
                        u_is2.wait(eng, jb + 1)
                        u_T2.wait(eng, jb)
                        u_xn2.inc(eng.activation(xn2b[:, jb % 2, :],
                                                 xb[:, jb % 4, :],
                                                 ACT.Identity,
                                                 bias=sn3[:, 2 + q:3 + q],
                                                 scale=sm3[:, 6 + q:7 + q]))
                    if 0 <= ja < nt:
                        u_Tq.wait(eng, ja + 1)
                        u_attn.wait(eng, ja - 1)
                        u_eTq.inc(eng.activation(qrT[:, ja % 2, :], p_tq[:, :],
                                                 ACT.Copy))

                @blk.tensor
                def _(eng):
                    if 0 <= j < nt:
                        u_xn1.wait(eng, j + 1)
                        u_eT2.wait(eng, j - 3)
                        for k in range(NK):
                            inst = eng.transpose(p_t2[:, k * P:(k + 1) * P],
                                                 xn1c[:, j % 2,
                                                      k * P:(k + 1) * P],
                                                 ident[:, :])
                        u_T1.inc(inst)
                    if 0 <= jb < nt:
                        if jb == 0:
                            t_lq.wait(eng, NK + 3 * NK)
                        u_eT1.wait(eng, min(jb + 2, nt))
                        u_pad.wait(eng, 1)
                        u_epi1.wait(eng, jb)
                        for h2 in range(2):
                            sl = slice(512 * h2, 512 * (h2 + 1))
                            first = True
                            for tap in range(3):
                                for k in range(NK):
                                    eng.matmul(
                                        p_lep[:, sl],
                                        xnT_all[:, k,
                                                jb * P + tap:jb * P + tap + P],
                                        Wlep[:, tap, k, sl],
                                        start=first, stop=False)
                                    first = False
                            inst = eng.matmul(p_lep[:, sl], ones1[:, :],
                                              lepb_row[:, sl],
                                              start=False, stop=True)
                        u_lepe.inc(inst)
                    if 0 <= jc < nt:
                        u_xn2.wait(eng, jc + 1)
                        u_eT1.wait(eng, min(jc + 4, nt))
                        for k in range(NK):
                            inst = eng.transpose(p_t2[:, k * P:(k + 1) * P],
                                                 xn2b[:, jc % 2,
                                                      k * P:(k + 1) * P],
                                                 ident[:, :])
                        u_T2.inc(inst)
                    if 0 <= jb < nt:
                        u_eTq.wait(eng, jb + 1)
                        u_lepe.wait(eng, jb + 1)
                        for h in range(H):
                            inst = eng.matmul(p_lep[:, h * P:(h + 1) * P],
                                              qrT[:, jb % 2, h * P:(h + 1) * P],
                                              kv_sb[:, h * P:(h + 1) * P],
                                              start=False, stop=True,
                                              skip_group_check=True)
                        u_attn.inc(inst)
                    if 0 <= jc < nt:
                        u_eT2.wait(eng, jc + 1)
                        u_x3.wait(eng, jc)
                        for h2 in range(2):
                            sl = slice(512 * h2, 512 * (h2 + 1))
                            for k in range(NK):
                                eng.matmul(p_y2[:, sl],
                                           xn2T[:, k * P:(k + 1) * P],
                                           Wc[:, k, sl],
                                           start=(k == 0), stop=False)
                            inst = eng.matmul(p_y2[:, sl], ones1[:, :],
                                              bc_row[:, sl],
                                              start=False, stop=True)
                        u_ffn2.inc(inst)
                    if 0 <= j < nt:
                        if j == 0:
                            t_wb.wait(eng, NK)
                        u_eT1.wait(eng, j + 1)
                        u_qev.wait(eng, j)
                        for h2 in range(2):
                            sl = slice(512 * h2, 512 * (h2 + 1))
                            for k in range(NK):
                                eng.matmul(p_qa[:, sl],
                                           xnT_all[:, k,
                                                   1 + j * P:1 + (j + 1) * P],
                                           Wqk_q[:, k, sl],
                                           start=(k == 0), stop=False)
                            inst = eng.matmul(p_qa[:, sl], ones1[:, :],
                                              qkbq_row[:, sl],
                                              start=False, stop=True)
                        u_qkq.inc(inst)
                    if 0 <= ja < nt:
                        u_rope.wait(eng, ja + 1)
                        u_eTq.wait(eng, ja)
                        for k in range(NK):
                            inst = eng.transpose(p_tq[:, k * P:(k + 1) * P],
                                                 qrb[:, k * P:(k + 1) * P],
                                                 ident[:, :])
                        u_Tq.inc(inst)

            if serial:
                for it in range(-1, nt + 3):
                    with nc.Block() as blk:
                        B_iter(blk, it)
                    nc.all_engine_barrier()
                blk = nc.Block().__enter__()
            else:
                blk = nc.Block().__enter__()
                for it in range(-1, nt + 3):
                    B_iter(blk, it)
            if True:
                @blk.sync
                def _(eng):
                    u_out.wait(eng, nt)
            blk.__exit__(None, None, None)

    return nc


_NC_CACHE = {}


def kernel(**inputs):
    T = inputs["L0_attn_x"].shape[1]
    B = inputs["L0_attn_x"].shape[0]
    key = (T,)
    if key not in _NC_CACHE:
        _NC_CACHE[key] = build_nc(T=T)
    nc = _NC_CACHE[key]
    names = ["L0_attn_qk_w", "L0_attn_qk_b", "L0_attn_lepe_w", "L0_attn_lepe_b",
             "L0_ffn_W", "L0_ffn_b", "L0_ffn_V_r", "L0_ffn_C"]
    shared = {n: np.ascontiguousarray(np.asarray(inputs[n], dtype=np.float32))
              for n in names}
    x = np.ascontiguousarray(np.asarray(inputs["L0_attn_x"], dtype=np.float32))
    in_maps = [dict(shared, L0_attn_x=x[i]) for i in range(B)]
    res = run_bass_kernel_spmd(nc, in_maps, core_ids=list(range(B)))
    return np.stack([np.asarray(r["out"], dtype=np.float32)
                     for r in res.results], axis=0)
